# version 22
# baseline (speedup 1.0000x reference)
import numpy as np
import concourse.bass as bass
import concourse.mybir as mybir
from concourse.bass_utils import run_bass_kernel_spmd

F32 = mybir.dt.float32
BF16 = mybir.dt.bfloat16
I32 = mybir.dt.int32
AF = mybir.ActivationFunctionType
ALU = mybir.AluOpType

NCORE = 8
T = 1024
KC = 16
D = 2048
MC = 64
GS = 8
NG = MC // GS
EPS = 1e-5
INV_D = 1.0 / D


def build_nc(dump_stages=()):
    nc = bass.Bass("TRN2", target_bir_lowering=False)

    u_in = nc.dram_tensor("u_in", [D, T], F32, kind="ExternalInput")
    U8 = mybir.dt.uint8
    wg_d = nc.dram_tensor("wg_d", [MC, 128, D], U8, kind="ExternalInput")
    wu_d = nc.dram_tensor("wu_d", [MC, 128, D], U8, kind="ExternalInput")
    wd_d = nc.dram_tensor("wd_d", [MC, 128, D], U8, kind="ExternalInput")
    wbr_d = nc.dram_tensor("wbr_d", [128, KC * 64], F32, kind="ExternalInput")
    wbs_d = nc.dram_tensor("wbs_d", [64, 256], F32, kind="ExternalInput")
    wcr_d = nc.dram_tensor("wcr_d", [128, 128], F32, kind="ExternalInput")
    wcm_d = nc.dram_tensor("wcm_d", [64, D], F32, kind="ExternalInput")
    dv_d = nc.dram_tensor("dv_d", [128, KC], F32, kind="ExternalInput")
    af_d = nc.dram_tensor("af_d", [1, 32], F32, kind="ExternalInput")
    gsc_d = nc.dram_tensor("gsc_d", [128, MC], F32, kind="ExternalInput")
    usc_d = nc.dram_tensor("usc_d", [128, MC], F32, kind="ExternalInput")
    dsc_d = nc.dram_tensor("dsc_d", [128, KC], F32, kind="ExternalInput")
    out_d = nc.dram_tensor("out_d", [D, T], F32, kind="ExternalOutput")

    from contextlib import ExitStack
    ctx = ExitStack()
    sb = lambda name, shape, dt: ctx.enter_context(nc.sbuf_tensor(name, shape, dt))
    ps = lambda name: ctx.enter_context(nc.psum_tensor(name, [128, 512], F32))

    x_sb = sb("x_sb", [128, KC * T], F32)
    xln2 = sb("xln2", [128, KC * T], BF16)
    wgbuf = sb("wgbuf", [128, 4 * D], BF16)
    wubuf = sb("wubuf", [128, 4 * D], BF16)
    wbr_sb = sb("wbr_sb", [128, KC * 64], BF16)
    wbs_sb = sb("wbs_sb", [64, 256], BF16)
    wcr_sb = sb("wcr_sb", [128, 128], BF16)
    wcm_sb = sb("wcm_sb", [64, D], BF16)
    dv_sb = sb("dv_sb", [128, KC], F32)
    gs_sb = sb("gs_sb", [128, MC], F32)
    us_sb = sb("us_sb", [128, MC], F32)
    ds_sb = sb("ds_sb", [128, KC], F32)
    af_sb = sb("af_sb", [1, 32], F32)
    bc_sb = sb("bc_sb", [128, 32], F32)
    ones128 = sb("ones128", [128, 1], BF16)
    ones1 = sb("ones1", [1, 128], F32)
    negones = sb("negones", [1, 128], F32)
    wb1_sb = sb("wb1_sb", [64, 1], F32)
    eps_sb = sb("eps_sb", [1, 1], F32)
    nhalf_sb = sb("nhalf_sb", [1, 1], F32)
    mean_sb = sb("mean_sb", [1, T], F32)
    var_sb = sb("var_sb", [1, T], F32)
    rstd_sb = sb("rstd_sb", [1, T], F32)
    negc = sb("negc", [128, D // 2], BF16)

    PG0, PG1 = ps("PG0"), ps("PG1")
    PU0, PU1 = ps("PU0"), ps("PU1")
    PD0, PD1 = ps("PD0"), ps("PD1")
    PM0, PM1 = ps("PM0"), ps("PM1")
    PG = (PG0, PG1)
    PU = (PU0, PU1)
    PD = (PD0, PD1)

    semnames = ["sdU0", "sdU1", "sdU2", "sdU3", "sdA",
                "sdG0", "sdG1", "sdG2", "sdG3", "sdW0", "sdW1", "sdW2", "sdW3",
                "sdD", "sO", "sK", "sKd", "sP0",
                "sT", "sS", "sV", "sTg", "sTu", "sSi", "sVh", "sTd", "sVd",
                "sPw", "sPd"]
    sems = {n: ctx.enter_context(nc.semaphore(n)) for n in semnames}
    cnt = {n: 0 for n in semnames}

    def inc(inst, name, k=1):
        inst.then_inc(sems[name], k)
        cnt[name] += k

    def wt(eng, name, v):
        eng.wait_ge(sems[name], v)

    def dump(name, src_ap, shape, dt):
        if name not in dump_stages:
            return
        d = nc.dram_tensor("dbg_" + name, shape, dt, kind="ExternalOutput")
        with nc.Block() as blk:
            @blk.sync
            def _(y):
                for nm in ("sT", "sS", "sV", "sTg", "sTu", "sSi", "sVh", "sTd", "sVd"):
                    if cnt[nm]:
                        wt(y, nm, cnt[nm])
                inc(y.dma_start(d[tuple(slice(None) for _ in shape)], src_ap), "sO", 16)
                wt(y, "sO", cnt["sO"])

    u_v = u_in[:, :].rearrange("(k p) s -> p k s", p=128)
    out_v = out_d[:, :].rearrange("(k p) s -> p k s", p=128)
    x_v = x_sb[:, :].rearrange("p (k s) -> p k s", s=T)

    with nc.Block() as blk:
        @blk.sync
        def _(y):
            for q in range(4):
                inc(y.dma_start(x_v[:, 4 * q:4 * q + 4, :], u_v[:, 4 * q:4 * q + 4, :]),
                    f"sdU{q}", 16)

        @blk.gpsimd
        def _(g):
            inc(g.dma_start(wbr_sb[:, :], wbr_d[:, :]), "sdA", 16)
            inc(g.dma_start(wbs_sb[:, :], wbs_d[:, :]), "sdA", 16)
            inc(g.dma_start(wcr_sb[:, :], wcr_d[:, :]), "sdA", 16)
            inc(g.dma_start(wcm_sb[:, :], wcm_d[:, :]), "sdA", 16)
            inc(g.dma_start(dv_sb[:, :], dv_d[:, :]), "sdA", 16)
            inc(g.dma_start(af_sb[:, :], af_d[:, :]), "sdA", 16)
            inc(g.dma_start(gs_sb[:, :], gsc_d[:, :]), "sdA", 16)
            inc(g.dma_start(us_sb[:, :], usc_d[:, :]), "sdA", 16)
            inc(g.dma_start(ds_sb[:, :], dsc_d[:, :]), "sdA", 16)
            for m in range(4):
                g.memset(wgbuf[:, (m % 4) * D:(m % 4 + 1) * D], -128.0)
                g.memset(wubuf[:, (m % 4) * D:(m % 4 + 1) * D], -128.0)
            inc(g.drain(), "sP0")
            wt(g, "sP0", 1)
            for m in range(4):
                inc(g.dma_start(wgbuf[:, (m % 4) * D:(m % 4 + 1) * D], wg_d[m, :, :],
                                accum_op=ALU.add), f"sdG{m % 4}", 16)
                inc(g.dma_start(wubuf[:, (m % 4) * D:(m % 4 + 1) * D], wu_d[m, :, :],
                                accum_op=ALU.add), f"sdW{m % 4}", 16)

        @blk.vector
        def _(v):
            v.memset(ones128[:, :], 1.0)
            v.memset(ones1[:, :], 1.0)
            v.memset(negones[:, :], -1.0)
            v.memset(eps_sb[:, :], EPS)
            v.memset(nhalf_sb[:, :], -0.5)
            v.memset(negc[:, :], -128.0)
            inc(v.drain(), "sV")


    v_const = cnt["sV"]

    ctxA = ExitStack()
    sbA = lambda name, shape, dt: ctxA.enter_context(nc.sbuf_tensor(name, shape, dt))
    uln = sbA("uln", [128, KC * T], BF16)
    xsq = sbA("xsq", [128, 2 * T], BF16)
    tmpn = sbA("tmpn", [128, 1024], BF16)
    t1_sb = sbA("t1_sb", [64, T], BF16)
    st_sb = sbA("st_sb", [128, 2 * T], BF16)
    st2_sb = sbA("st2_sb", [128, 2 * T], BF16)
    t3_sb = sbA("t3_sb", [64, T], BF16)
    kr0 = sbA("kr0", [128, 128], F32)
    kr1 = sbA("kr1", [128, 128], F32)
    if dump_stages:
        k5sd = sbA("k5sd", [32, 32], F32)
    bMb = sbA("bMb", [128, 1024], BF16)
    bRb = sbA("bRb", [128, 1024], BF16)
    mbf0 = sbA("mbf0", [128, 256], BF16)
    mbf1 = sbA("mbf1", [128, 256], BF16)

    def emit_ln(out_tile, with_bc, cast_wait, xbf_on_s=False, t_pre_wait=None,
                xbf_ready=None):
        tT0, tS0, tV0 = cnt["sT"], cnt["sS"], cnt["sV"]
        dt_bc = 1 if with_bc else 0
        vc = 0 if (xbf_on_s or xbf_ready is not None) else KC
        spc = 2 if xbf_on_s else 1
        if with_bc:
            bcM0, bcM1, bcR0, bcR1 = PM0, PM1, PU0, PU1
            sx0, sx1, sq0, sq1 = PM0, PM1, PD0, PD1
        else:
            bcM0, bcM1, bcR0, bcR1 = PM0, PM1, PD0, PD1
            sx0, sx1, sq0, sq1 = PG0, PG1, PM0, PM1
        nrm_base = tV0 + dt_bc + vc + 1 + (1 if with_bc else 0)
        with nc.Block() as blk:
            if with_bc:
                @blk.gpsimd
                def _(g):
                    wt(g, "sK", 1)
                    inc(g.dma_start(kr0[0:32, 0:32], mean_sb[0:1, 0:1024]), "sKd", 16)

            @blk.tensor
            def _(t):
                if with_bc:
                    wt(t, "sdA", cnt["sdA"])
                    wt(t, "sV", v_const)
                    inc(t.matmul(PM0[:, 0:32], ones1[:, :], af_sb[:, :],
                                 start=True, stop=True), "sT")
                    wt(t, "sV", tV0 + 1)
                if t_pre_wait is not None:
                    wt(t, t_pre_wait[0], t_pre_wait[1])
                for k in range(KC):
                    if xbf_ready is not None:
                        wt(t, *xbf_ready(k))
                    elif not xbf_on_s:
                        wt(t, "sV", tV0 + dt_bc + k + 1)
                    wt(t, "sS", tS0 + spc * (k + 1))
                    par = k % 2
                    st, sp = (k == 0), (k == KC - 1)
                    xb = out_tile[:, k * T:(k + 1) * T]
                    t.matmul(sx0[0:1, :], ones128[:, :], xb[:, 0:512],
                             start=st, stop=sp)
                    t.matmul(sx1[0:1, :], ones128[:, :], xb[:, 512:1024],
                             start=st, stop=sp)
                    t.matmul(sq0[0:1, :], ones128[:, :], xsq[:, par * T:par * T + 512],
                             start=st, stop=sp)
                    last = t.matmul(sq1[0:1, :], ones128[:, :],
                                    xsq[:, par * T + 512:par * T + 1024],
                                    start=st, stop=sp)
                    if with_bc:
                        t.matmul(PG0[0:64, :], wbr_sb[:, k * 64:(k + 1) * 64],
                                 xb[:, 0:512], start=st, stop=sp)
                        t.matmul(PG1[0:64, :], wbr_sb[:, k * 64:(k + 1) * 64],
                                 xb[:, 512:1024], start=st, stop=sp)
                        last = t.matmul(PU0[0:64, 0:1],
                                        wbr_sb[:, k * 64:(k + 1) * 64],
                                        ones128[:, 0:1], start=st, stop=sp)
                    inc(last, "sT")
                wt(t, "sV", tV0 + dt_bc + vc + 1)
                wt(t, "sS", tS0 + spc * KC + 1)
                t.matmul(bcM0[:, :], negones[:, :], mean_sb[0:1, 0:512], start=True, stop=True)
                t.matmul(bcM1[:, :], negones[:, :], mean_sb[0:1, 512:1024], start=True, stop=True)
                t.matmul(bcR0[:, :], ones1[:, :], rstd_sb[0:1, 0:512], start=True, stop=True)
                inc(t.matmul(bcR1[:, :], ones1[:, :], rstd_sb[0:1, 512:1024],
                             start=True, stop=True), "sT")

            @blk.scalar
            def _(s):
                for k in range(KC):
                    for nm, vv in cast_wait(k):
                        wt(s, nm, vv)
                    if k >= 2:
                        wt(s, "sT", tT0 + dt_bc + k - 1)
                    inc(s.activation(xsq[:, (k % 2) * T:(k % 2 + 1) * T],
                                     x_sb[:, k * T:(k + 1) * T], AF.Square), "sS")
                    if xbf_on_s:
                        inc(s.copy(xbf[:, (k % 2) * T:(k % 2 + 1) * T],
                                   x_sb[:, k * T:(k + 1) * T]), "sS")
                wt(s, "sV", tV0 + dt_bc + vc + 1)
                s.activation(var_sb[0:1, :], var_sb[0:1, :], AF.Ln,
                             bias=eps_sb[0:1, :])
                s.drain()
                inc(s.activation(rstd_sb[0:1, :], var_sb[0:1, :], AF.Exp,
                                 scale=nhalf_sb[0:1, :]), "sS")

            @blk.vector
            def _(v):
                if with_bc:
                    wt(v, "sT", tT0 + 1)
                    v.tensor_copy(bc_sb[:, :], PM0[:, 0:32])
                    inc(v.drain(), "sV")
                    v.tensor_copy(mean_sb[0:1, 0:4], bc_sb[0:1, 28:32])
                    v.drain()
                    src_f, dst_f = mean_sb, var_sb
                    li = None
                    for tlv in range(1, 5):
                        m = 1 << tlv
                        lvl = 7 - tlv
                        srcv = src_f[0:1, 0:m * m].rearrange("o (r c) -> o r c", c=m)
                        dstv = dst_f[0:1, 0:4 * m * m].rearrange(
                            "o (i r j c) -> o i r j c", i=2, r=m, j=2, c=m)
                        for i in range(2):
                            for j in range(2):
                                col = lvl * 4 + i * 2 + j
                                li = v.tensor_scalar_mul(dstv[0:1, i, :, j, :],
                                                         srcv[0:1, :, :],
                                                         bc_sb[0:1, col:col + 1])
                        v.drain()
                        src_f, dst_f = dst_f, src_f
                    inc(v.drain(), "sK")
                    wt(v, "sKd", 16)
                    if dump_stages:
                        v.tensor_copy(k5sd[:, :], kr0[0:32, 0:32])
                    src2, dst2 = kr0, kr1
                    for tlv in (5, 6):
                        m = 1 << tlv
                        lvl = 7 - tlv
                        for i in range(2):
                            for j in range(2):
                                col = lvl * 4 + i * 2 + j
                                v.tensor_scalar_mul(
                                    dst2[i * m:(i + 1) * m, j * m:(j + 1) * m],
                                    src2[0:m, 0:m], bc_sb[i * m:(i + 1) * m, col:col + 1])
                        v.drain()
                        src2, dst2 = dst2, src2
                    for j in range(2):
                        v.tensor_scalar_mul(mbf0[:, j * 128:(j + 1) * 128], src2[:, :],
                                            bc_sb[:, j:j + 1])
                        v.tensor_scalar_mul(mbf1[:, j * 128:(j + 1) * 128],
                                            src2[:, :], bc_sb[:, 2 + j:2 + j + 1])
                    inc(v.drain(), "sK")
                if not xbf_on_s and xbf_ready is None:
                    for k in range(KC):
                        for nm, vv in cast_wait(k):
                            wt(v, nm, vv)
                        inc(v.tensor_copy(out_tile[:, k * T:(k + 1) * T],
                                          x_sb[:, k * T:(k + 1) * T]), "sV")
                wt(v, "sT", tT0 + dt_bc + KC)
                v.tensor_scalar_mul(mean_sb[0:1, 0:512], sx0[0:1, :], INV_D)
                v.tensor_scalar_mul(mean_sb[0:1, 512:1024], sx1[0:1, :], INV_D)
                v.tensor_scalar_mul(var_sb[0:1, 0:512], sq0[0:1, :], INV_D)
                v.tensor_scalar_mul(var_sb[0:1, 512:1024], sq1[0:1, :], INV_D)
                v.drain()
                v.tensor_mul(rstd_sb[0:1, :], mean_sb[0:1, :], mean_sb[0:1, :])
                if with_bc:
                    v.tensor_copy(wb1_sb[0:64, 0:1], PU0[0:64, 0:1])
                v.drain()
                v.tensor_sub(var_sb[0:1, :], var_sb[0:1, :], rstd_sb[0:1, :])
                inc(v.drain(), "sV")
                wt(v, "sT", tT0 + dt_bc + KC + 1)
                v.tensor_copy(bMb[:, 0:512], bcM0[:, :])
                v.tensor_copy(bMb[:, 512:1024], bcM1[:, :])
                v.tensor_copy(bRb[:, 0:512], bcR0[:, :])
                v.tensor_copy(bRb[:, 512:1024], bcR1[:, :])
                v.drain()
                if with_bc:
                    for hh in range(2):
                        v.scalar_tensor_tensor(
                            tmpn[0:64, hh * 512:(hh + 1) * 512],
                            bMb[0:64, hh * 512:(hh + 1) * 512], wb1_sb[0:64, 0:1],
                            (PG0 if hh == 0 else PG1)[0:64, :], ALU.mult, ALU.add)
                    v.drain()
                    for hh in range(2):
                        v.tensor_mul(t1_sb[0:64, hh * 512:(hh + 1) * 512],
                                     tmpn[0:64, hh * 512:(hh + 1) * 512],
                                     bRb[0:64, hh * 512:(hh + 1) * 512])
                    inc(v.drain(), "sV")
                for k in range(KC):
                    xin_k = (out_tile[:, k * T:(k + 1) * T] if xbf_ready is not None
                             else x_sb[:, k * T:(k + 1) * T])
                    v.tensor_add(tmpn[:, :], xin_k, bMb[:, :])
                    v.drain()
                    v.tensor_mul(out_tile[:, k * T:(k + 1) * T], tmpn[:, :], bRb[:, :])
                    inc(v.drain(), "sV")
        return nrm_base

    ln1_nrm = emit_ln(uln, with_bc=True,
                      cast_wait=lambda k: [(f"sdU{k // 4}", 16)])
    ln1_fix = ln1_nrm
    dump("uln", uln[:, :], [128, KC * T], BF16)
    if dump_stages:
        dump("k5s", k5sd[:, :], [32, 32], F32)
    dump("bc", bc_sb[:, :], [128, 32], F32)
    dump("k7", kr0[:, :], [128, 128], F32)
    dump("k6", kr1[:, :], [128, 128], F32)
    dump("mf0", mbf0[:, :], [128, 256], BF16)
    dump("mf1", mbf1[:, :], [128, 256], BF16)
    dump("mean", mean_sb[0:1, :], [1, T], F32)
    dump("rstd", rstd_sb[0:1, :], [1, T], F32)

    with nc.Block() as blk:
        tT0, tS0, tV0 = cnt["sT"], cnt["sS"], cnt["sV"]

        @blk.tensor
        def _(t):
            wt(t, "sK", 2)
            wt(t, "sV", ln1_fix)
            pmap = {(0, 0): PD0, (0, 1): PD1, (1, 0): PM0, (1, 1): PM1}
            for n0 in range(2):
                for hh in range(2):
                    inc(t.matmul(pmap[(n0, hh)][:, :],
                                 wbs_sb[0:64, n0 * 128:(n0 + 1) * 128],
                                 t1_sb[0:64, hh * 512:(hh + 1) * 512],
                                 start=True, stop=True), "sT")
            wt(t, "sS", tS0 + 4)
            pmap2 = {(0, 0): PG0, (0, 1): PG1, (1, 0): PD0, (1, 1): PD1}
            for j0 in range(2):
                for hh in range(2):
                    for i0 in range(2):
                        mb = mbf0 if i0 == 0 else mbf1
                        last = t.matmul(pmap2[(j0, hh)][:, :],
                                        mb[:, j0 * 128:(j0 + 1) * 128],
                                        st_sb[:, i0 * T + hh * 512:i0 * T + (hh + 1) * 512],
                                        start=(i0 == 0), stop=(i0 == 1))
                    inc(last, "sT")
            wt(t, "sS", tS0 + 8)
            for hh in range(2):
                for j0 in range(2):
                    last = t.matmul((PM0 if hh == 0 else PM1)[0:64, :],
                                    wcr_sb[:, j0 * 64:(j0 + 1) * 64],
                                    st2_sb[:, j0 * T + hh * 512:j0 * T + (hh + 1) * 512],
                                    start=(j0 == 0), stop=(j0 == 1))
                inc(last, "sT")
            wt(t, "sS", tS0 + 10)
            ybanks = (PD0, PD1, PG0, PG1)
            for c in range(KC):
                for hh in range(2):
                    s_ = 2 * c + hh
                    if s_ >= 4:
                        wt(t, "sV", tV0 + s_ - 3)
                    inc(t.matmul(ybanks[s_ % 4][:, :],
                                 wcm_sb[0:64, c * 128:(c + 1) * 128],
                                 t3_sb[0:64, hh * 512:(hh + 1) * 512],
                                 start=True, stop=True), "sT")

        @blk.scalar
        def _(s):
            pmap = {(0, 0): PD0, (0, 1): PD1, (1, 0): PM0, (1, 1): PM1}
            i = 0
            for n0 in range(2):
                for hh in range(2):
                    i += 1
                    wt(s, "sT", tT0 + i)
                    inc(s.copy(st_sb[:, n0 * T + hh * 512:n0 * T + (hh + 1) * 512],
                               pmap[(n0, hh)][:, :]), "sS")
            pmap2 = {(0, 0): PG0, (0, 1): PG1, (1, 0): PD0, (1, 1): PD1}
            i = 0
            for j0 in range(2):
                for hh in range(2):
                    i += 1
                    wt(s, "sT", tT0 + 4 + i)
                    inc(s.copy(st2_sb[:, j0 * T + hh * 512:j0 * T + (hh + 1) * 512],
                               pmap2[(j0, hh)][:, :]), "sS")
            for hh in range(2):
                wt(s, "sT", tT0 + 9 + hh)
                inc(s.copy(t3_sb[0:64, hh * 512:(hh + 1) * 512],
                           (PM0 if hh == 0 else PM1)[0:64, :]), "sS")
            for c in range(KC):
                wt(s, "sV", tV0 + 2 * c + 2)
                inc(s.copy(xln2[:, c * T:(c + 1) * T],
                           x_sb[:, c * T:(c + 1) * T]), "sS")

        @blk.vector
        def _(v):
            wt(v, "sdA", 144)
            ybanks = (PD0, PD1, PG0, PG1)
            for c in range(KC):
                wt(v, "sT", tT0 + 11 + 2 * c + 1)
                for hh in range(2):
                    xs = x_sb[:, c * T + hh * 512:c * T + (hh + 1) * 512]
                    v.tensor_add(xs, xs, ybanks[(2 * c + hh) % 4][:, :])
                v.drain()
                for hh in range(2):
                    xs = x_sb[:, c * T + hh * 512:c * T + (hh + 1) * 512]
                    us = uln[:, c * T + hh * 512:c * T + (hh + 1) * 512]
                    inc(v.scalar_tensor_tensor(xs, us, dv_sb[:, c:c + 1], xs,
                                               ALU.mult, ALU.add), "sV")

    dump("xattn", x_sb[:, :], [128, KC * T], F32)
    dump("t1", t1_sb[0:64, :], [64, T], BF16)
    dump("st", st_sb[:, :], [128, 2 * T], BF16)
    dump("st2", st2_sb[:, :], [128, 2 * T], BF16)
    dump("t3", t3_sb[0:64, :], [64, T], BF16)

    p2vb = cnt["sV"] - 32
    p2sb = cnt["sS"] - 26
    ln2_nrm = emit_ln(xln2, with_bc=False,
                      cast_wait=lambda k: [("sV", p2vb + 2 * k + 2)],
                      xbf_ready=lambda k: ("sS", p2sb + 11 + k))
    dump("xln2", xln2[:, :], [128, KC * T], BF16)

    ctxA.close()
    ctxB = ExitStack()
    sbB = lambda name, shape, dt: ctxB.enter_context(nc.sbuf_tensor(name, shape, dt))
    h_sb = sbB("h_sb", [128, GS * T], BF16)
    sg_sb = sbB("sg_sb", [128, 2 * 512], F32)
    wd_grp = sbB("wd_grp", [128, GS * D], BF16)

    for g in range(NG):
        with nc.Block() as blk:
            @blk.gpsimd
            def _(gp):
                wt(gp, "sPd", g + 1)
                wdv = wd_d[g * GS:(g + 1) * GS, :, :].rearrange("g p h -> p g h")
                wgv = wd_grp[:, :].rearrange("p (g h) -> p g h", h=D)
                inc(gp.dma_start(wgv[:, 0:4, :], wdv[:, 0:4, :], accum_op=ALU.add),
                    "sdD", 16)
                inc(gp.dma_start(wgv[:, 4:8, :], wdv[:, 4:8, :], accum_op=ALU.add),
                    "sdD", 16)
                for mp in range(8 * g + 4, min(8 * g + 10, MC)):
                    wt(gp, "sPw", mp - 3)
                    inc(gp.dma_start(wgbuf[:, (mp % 4) * D:(mp % 4 + 1) * D],
                                     wg_d[mp, :, :], accum_op=ALU.add),
                        f"sdG{mp % 4}", 16)
                    inc(gp.dma_start(wubuf[:, (mp % 4) * D:(mp % 4 + 1) * D],
                                     wu_d[mp, :, :], accum_op=ALU.add),
                        f"sdW{mp % 4}", 16)

            @blk.tensor
            def _(t):
                if g > 0:
                    wt(t, "sVd", 32 * g)
                if g == 0:
                    wt(t, "sdG0", 16)
                    wt(t, "sdW0", 16)
                    for k in range(KC):
                        wt(t, "sV", ln2_nrm + k + 1)
                        st, sp = (k == 0), (k == KC - 1)
                        for hh in range(2):
                            xs = xln2[:, k * T + hh * 512:k * T + (hh + 1) * 512]
                            gm = t.matmul(PG[hh][:, :],
                                          wgbuf[:, k * 128:(k + 1) * 128],
                                          xs, start=st, stop=sp)
                            um = t.matmul(PU[hh][:, :],
                                          wubuf[:, k * 128:(k + 1) * 128],
                                          xs, start=st, stop=sp)
                            if sp:
                                inc(gm, "sTg")
                                inc(um, "sTu")
                for ml in range(GS):
                    m = g * GS + ml
                    if g == 0 and ml == 0:
                        continue
                    for hh in range(2):
                        sg_i = 2 * m + hh
                        par = sg_i % 2
                        if sg_i >= 2:
                            wt(t, "sSi", sg_i - 1)
                            wt(t, "sVh", sg_i - 1)
                        if hh == 0:
                            wt(t, f"sdG{m % 4}", 16 * (m // 4 + 1))
                            wt(t, f"sdW{m % 4}", 16 * (m // 4 + 1))
                        for k in range(KC):
                            mm = t.matmul(PG[par][:, :],
                                          wgbuf[:, (m % 4) * D + k * 128:(m % 4) * D + (k + 1) * 128],
                                          xln2[:, k * T + hh * 512:k * T + (hh + 1) * 512],
                                          start=(k == 0), stop=(k == KC - 1))
                        inc(mm, "sTg")
                        for k in range(KC):
                            mm = t.matmul(PU[par][:, :],
                                          wubuf[:, (m % 4) * D + k * 128:(m % 4) * D + (k + 1) * 128],
                                          xln2[:, k * T + hh * 512:k * T + (hh + 1) * 512],
                                          start=(k == 0), stop=(k == KC - 1))
                        inc(mm, "sTu")

            @blk.scalar
            def _(s):
                if g == 0:
                    wt(s, "sdA", 144)
                for ml in range(GS):
                    m = g * GS + ml
                    for hh in range(2):
                        sg_i = 2 * m + hh
                        par = sg_i % 2
                        wt(s, "sTg", sg_i + 1)
                        inc(s.activation(sg_sb[:, par * 512:(par + 1) * 512],
                                         PG[par][:, :], AF.Silu,
                                         scale=gs_sb[:, m:m + 1]), "sSi")

            @blk.vector
            def _(v):
                if g > 0:
                    wt(v, "sTd", cnt["sTd"])
                for j in range(2 * GS):
                    v.tensor_max(wd_grp[:, j * (D // 2):(j + 1) * (D // 2)],
                                 negc[:, :], negc[:, :])
                inc(v.drain(), "sPd")
                for ml in range(GS):
                    m = g * GS + ml
                    for hh in range(2):
                        sg_i = 2 * m + hh
                        par = sg_i % 2
                        wt(v, "sTu", sg_i + 1)
                        wt(v, "sSi", sg_i + 1)
                        inc(v.scalar_tensor_tensor(
                            h_sb[:, ml * T + hh * 512:ml * T + (hh + 1) * 512],
                            PU[par][:, :], us_sb[:, m:m + 1],
                            sg_sb[:, par * 512:(par + 1) * 512],
                            ALU.mult, ALU.mult), "sVh")
                        if hh == 1 and ml < 6:
                            mp = m + 4
                            if mp < MC:
                                for bb in (wgbuf, wubuf):
                                    for q2 in range(2):
                                        v.tensor_max(
                                            bb[:, (mp % 4) * D + q2 * (D // 2):
                                               (mp % 4) * D + (q2 + 1) * (D // 2)],
                                            negc[:, :], negc[:, :])
                                inc(v.drain(), "sPw")

        with nc.Block() as blk:
            tTd0, tVd0 = cnt["sTd"], cnt["sVd"]

            @blk.gpsimd
            def _(gp):
                for mp in range(8 * g + 10, min(8 * g + 12, MC)):
                    wt(gp, "sPw", mp - 3)
                    inc(gp.dma_start(wgbuf[:, (mp % 4) * D:(mp % 4 + 1) * D],
                                     wg_d[mp, :, :], accum_op=ALU.add),
                        f"sdG{mp % 4}", 16)
                    inc(gp.dma_start(wubuf[:, (mp % 4) * D:(mp % 4 + 1) * D],
                                     wu_d[mp, :, :], accum_op=ALU.add),
                        f"sdW{mp % 4}", 16)

            @blk.tensor
            def _(t):
                wt(t, "sdD", cnt["sdD"])
                wt(t, "sVh", 16 * (g + 1))
                PDX = (PD0, PD1, PG0, PG1)
                for c in range(KC):
                    for hh in range(2):
                        sl = 2 * c + hh
                        if sl >= 4:
                            wt(t, "sVd", tVd0 + sl - 3)
                        for ml in range(GS):
                            mm = t.matmul(PDX[sl % 4][:, :],
                                          wd_grp[:, ml * D + c * 128:ml * D + (c + 1) * 128],
                                          h_sb[:, ml * T + hh * 512:ml * T + (hh + 1) * 512],
                                          start=(ml == 0), stop=(ml == GS - 1))
                        inc(mm, "sTd")

            @blk.vector
            def _(v):
                PDX = (PD0, PD1, PG0, PG1)
                for c in range(KC):
                    for hh in range(2):
                        sl = 2 * c + hh
                        wt(v, "sTd", tTd0 + sl + 1)
                        xs = x_sb[:, c * T + hh * 512:c * T + (hh + 1) * 512]
                        inc(v.scalar_tensor_tensor(xs, PDX[sl % 4][:, :],
                                                   ds_sb[:, c:c + 1], xs,
                                                   ALU.mult, ALU.add), "sVd")
                    if c in (1, 3):
                        mp = 8 * g + 10 + (c - 1) // 2
                        if mp < MC:
                            for bb in (wgbuf, wubuf):
                                for q2 in range(2):
                                    v.tensor_max(
                                        bb[:, (mp % 4) * D + q2 * (D // 2):
                                           (mp % 4) * D + (q2 + 1) * (D // 2)],
                                        negc[:, :], negc[:, :])
                            inc(v.drain(), "sPw")
                if g == NG - 1:
                    inc(v.drain(), "sVd")

        if g == 0:
            dump("h0", h_sb[:, :], [128, GS * T], BF16)
            dump("x0", x_sb[:, :], [128, KC * T], F32)

    ctxB.close()

    with nc.Block() as blk:
        @blk.sync
        def _(y):
            wt(y, "sVd", cnt["sVd"])
            for q in range(4):
                inc(y.dma_start(out_v[:, 4 * q:4 * q + 4, :], x_v[:, 4 * q:4 * q + 4, :]),
                    "sO", 16)
            wt(y, "sO", cnt["sO"])

    ctx.close()
    return nc


_NC_CACHE = None


def kernel(L0_attn_u, L0_attn_A_factors, L0_attn_W_b_rank, L0_attn_W_b_state,
           L0_attn_W_c_rank, L0_attn_W_c_model, L0_attn_D_vec,
           L0_ffn_gate_w, L0_ffn_gate_scale, L0_ffn_up_w, L0_ffn_up_scale,
           L0_ffn_down_w, L0_ffn_down_scale):
    global _NC_CACHE
    u = np.asarray(L0_attn_u)
    uT = np.ascontiguousarray(u[0].T)
    gate = np.asarray(L0_ffn_gate_w)
    up = np.asarray(L0_ffn_up_w)
    down = np.asarray(L0_ffn_down_w)
    def lobyte(a):
        return np.ascontiguousarray(
            a.view(np.uint8).reshape(a.shape + (4,))[..., 0])
    gate_p = lobyte(np.ascontiguousarray(
        gate.T.reshape(KC, 128, MC, 128).transpose(2, 1, 0, 3)).reshape(MC, 128, D))
    up_p = lobyte(np.ascontiguousarray(
        up.T.reshape(KC, 128, MC, 128).transpose(2, 1, 0, 3)).reshape(MC, 128, D))
    down_p = lobyte(np.ascontiguousarray(down.T).reshape(MC, 128, D))
    wbr_p = np.ascontiguousarray(
        np.asarray(L0_attn_W_b_rank).reshape(KC, 128, 64).transpose(1, 0, 2)
    ).reshape(128, KC * 64)
    wbs_p = np.ascontiguousarray(np.asarray(L0_attn_W_b_state))
    wcr_p = np.ascontiguousarray(
        np.asarray(L0_attn_W_c_rank).reshape(2, 128, 64).transpose(1, 0, 2)
    ).reshape(128, 128)
    wcm_p = np.ascontiguousarray(np.asarray(L0_attn_W_c_model))
    dv_p = np.ascontiguousarray(np.asarray(L0_attn_D_vec).reshape(KC, 128).T)
    af_p = np.ascontiguousarray(np.asarray(L0_attn_A_factors).reshape(1, 32))
    gs_p = np.ascontiguousarray(np.asarray(L0_ffn_gate_scale).reshape(MC, 128).T)
    us_p = np.ascontiguousarray(np.asarray(L0_ffn_up_scale).reshape(MC, 128).T)
    ds_p = np.ascontiguousarray(np.asarray(L0_ffn_down_scale).reshape(KC, 128).T)

    if _NC_CACHE is None:
        _NC_CACHE = build_nc()
    nc = _NC_CACHE

    shared = {
        "wg_d": gate_p, "wu_d": up_p, "wd_d": down_p,
        "wbr_d": wbr_p, "wbs_d": wbs_p, "wcr_d": wcr_p, "wcm_d": wcm_p,
        "dv_d": dv_p, "af_d": af_p, "gsc_d": gs_p, "usc_d": us_p, "dsc_d": ds_p,
    }
    in_maps = []
    for c in range(NCORE):
        m = dict(shared)
        m["u_in"] = np.ascontiguousarray(uT[:, c * T:(c + 1) * T])
        in_maps.append(m)

    res = run_bass_kernel_spmd(nc, in_maps, core_ids=list(range(NCORE)))
    outT = np.concatenate([res.results[c]["out_d"] for c in range(NCORE)], axis=1)
    return np.ascontiguousarray(outT.T).reshape(1, 8192, 2048).astype(np.float32)


# version 23
# speedup vs baseline: 1.0147x; 1.0147x over previous
import numpy as np
import concourse.bass as bass
import concourse.mybir as mybir
from concourse.bass_utils import run_bass_kernel_spmd

F32 = mybir.dt.float32
BF16 = mybir.dt.bfloat16
I32 = mybir.dt.int32
AF = mybir.ActivationFunctionType
ALU = mybir.AluOpType

NCORE = 8
T = 1024
KC = 16
D = 2048
MC = 64
GS = 8
NG = MC // GS
EPS = 1e-5
INV_D = 1.0 / D


def build_nc(dump_stages=()):
    nc = bass.Bass("TRN2", target_bir_lowering=False)

    u_in = nc.dram_tensor("u_in", [D, T], F32, kind="ExternalInput")
    U8 = mybir.dt.uint8
    wg_d = nc.dram_tensor("wg_d", [MC, 128, D], U8, kind="ExternalInput")
    wu_d = nc.dram_tensor("wu_d", [MC, 128, D], U8, kind="ExternalInput")
    wd_d = nc.dram_tensor("wd_d", [MC, 128, D], U8, kind="ExternalInput")
    wbr_d = nc.dram_tensor("wbr_d", [128, KC * 64], F32, kind="ExternalInput")
    wbs_d = nc.dram_tensor("wbs_d", [64, 256], F32, kind="ExternalInput")
    wcr_d = nc.dram_tensor("wcr_d", [128, 128], F32, kind="ExternalInput")
    wcm_d = nc.dram_tensor("wcm_d", [64, D], F32, kind="ExternalInput")
    dv_d = nc.dram_tensor("dv_d", [128, KC], F32, kind="ExternalInput")
    af_d = nc.dram_tensor("af_d", [1, 32], F32, kind="ExternalInput")
    gsc_d = nc.dram_tensor("gsc_d", [128, MC], F32, kind="ExternalInput")
    usc_d = nc.dram_tensor("usc_d", [128, MC], F32, kind="ExternalInput")
    dsc_d = nc.dram_tensor("dsc_d", [128, KC], F32, kind="ExternalInput")
    out_d = nc.dram_tensor("out_d", [D, T], F32, kind="ExternalOutput")

    from contextlib import ExitStack
    ctx = ExitStack()
    sb = lambda name, shape, dt: ctx.enter_context(nc.sbuf_tensor(name, shape, dt))
    ps = lambda name: ctx.enter_context(nc.psum_tensor(name, [128, 512], F32))

    x_sb = sb("x_sb", [128, KC * T], F32)
    xln2 = sb("xln2", [128, KC * T], BF16)
    wgbuf = sb("wgbuf", [128, 4 * D], BF16)
    wubuf = sb("wubuf", [128, 4 * D], BF16)
    wbr_sb = sb("wbr_sb", [128, KC * 64], BF16)
    wbs_sb = sb("wbs_sb", [64, 256], BF16)
    wcr_sb = sb("wcr_sb", [128, 128], BF16)
    wcm_sb = sb("wcm_sb", [64, D], BF16)
    dv_sb = sb("dv_sb", [128, KC], F32)
    gs_sb = sb("gs_sb", [128, MC], F32)
    us_sb = sb("us_sb", [128, MC], F32)
    ds_sb = sb("ds_sb", [128, KC], F32)
    af_sb = sb("af_sb", [1, 32], F32)
    bc_sb = sb("bc_sb", [128, 32], F32)
    ones128 = sb("ones128", [128, 1], BF16)
    ones1 = sb("ones1", [1, 128], F32)
    negones = sb("negones", [1, 128], F32)
    wb1_sb = sb("wb1_sb", [64, 1], F32)
    eps_sb = sb("eps_sb", [1, 1], F32)
    nhalf_sb = sb("nhalf_sb", [1, 1], F32)
    mean_sb = sb("mean_sb", [1, T], F32)
    var_sb = sb("var_sb", [1, T], F32)
    rstd_sb = sb("rstd_sb", [1, T], F32)
    negc = sb("negc", [128, D // 2], BF16)

    PG0, PG1 = ps("PG0"), ps("PG1")
    PU0, PU1 = ps("PU0"), ps("PU1")
    PD0, PD1 = ps("PD0"), ps("PD1")
    PM0, PM1 = ps("PM0"), ps("PM1")
    PG = (PG0, PG1)
    PU = (PU0, PU1)
    PD = (PD0, PD1)

    semnames = ["sdU0", "sdU1", "sdU2", "sdU3", "sdA",
                "sdG0", "sdG1", "sdG2", "sdG3", "sdW0", "sdW1", "sdW2", "sdW3",
                "sdD", "sO", "sK", "sKd", "sP0",
                "sT", "sS", "sV", "sTg", "sTu", "sSi", "sVh", "sTd", "sVd",
                "sPw", "sPd"]
    sems = {n: ctx.enter_context(nc.semaphore(n)) for n in semnames}
    cnt = {n: 0 for n in semnames}

    def inc(inst, name, k=1):
        inst.then_inc(sems[name], k)
        cnt[name] += k

    def wt(eng, name, v):
        eng.wait_ge(sems[name], v)

    def dump(name, src_ap, shape, dt):
        if name not in dump_stages:
            return
        d = nc.dram_tensor("dbg_" + name, shape, dt, kind="ExternalOutput")
        with nc.Block() as blk:
            @blk.sync
            def _(y):
                for nm in ("sT", "sS", "sV", "sTg", "sTu", "sSi", "sVh", "sTd", "sVd"):
                    if cnt[nm]:
                        wt(y, nm, cnt[nm])
                inc(y.dma_start(d[tuple(slice(None) for _ in shape)], src_ap), "sO", 16)
                wt(y, "sO", cnt["sO"])

    u_v = u_in[:, :].rearrange("(k p) s -> p k s", p=128)
    out_v = out_d[:, :].rearrange("(k p) s -> p k s", p=128)
    x_v = x_sb[:, :].rearrange("p (k s) -> p k s", s=T)

    with nc.Block() as blk:
        @blk.sync
        def _(y):
            for q in range(4):
                inc(y.dma_start(x_v[:, 4 * q:4 * q + 4, :], u_v[:, 4 * q:4 * q + 4, :]),
                    f"sdU{q}", 16)

        @blk.gpsimd
        def _(g):
            inc(g.dma_start(wbr_sb[:, :], wbr_d[:, :]), "sdA", 16)
            inc(g.dma_start(wbs_sb[:, :], wbs_d[:, :]), "sdA", 16)
            inc(g.dma_start(wcr_sb[:, :], wcr_d[:, :]), "sdA", 16)
            inc(g.dma_start(wcm_sb[:, :], wcm_d[:, :]), "sdA", 16)
            inc(g.dma_start(dv_sb[:, :], dv_d[:, :]), "sdA", 16)
            inc(g.dma_start(af_sb[:, :], af_d[:, :]), "sdA", 16)
            inc(g.dma_start(gs_sb[:, :], gsc_d[:, :]), "sdA", 16)
            inc(g.dma_start(us_sb[:, :], usc_d[:, :]), "sdA", 16)
            inc(g.dma_start(ds_sb[:, :], dsc_d[:, :]), "sdA", 16)
            for m in range(4):
                g.memset(wgbuf[:, (m % 4) * D:(m % 4 + 1) * D], -128.0)
                g.memset(wubuf[:, (m % 4) * D:(m % 4 + 1) * D], -128.0)
            inc(g.drain(), "sP0")
            wt(g, "sP0", 1)
            for m in range(4):
                inc(g.dma_start(wgbuf[:, (m % 4) * D:(m % 4 + 1) * D], wg_d[m, :, :],
                                accum_op=ALU.add), f"sdG{m % 4}", 16)
                inc(g.dma_start(wubuf[:, (m % 4) * D:(m % 4 + 1) * D], wu_d[m, :, :],
                                accum_op=ALU.add), f"sdW{m % 4}", 16)

        @blk.vector
        def _(v):
            v.memset(ones128[:, :], 1.0)
            v.memset(ones1[:, :], 1.0)
            v.memset(negones[:, :], -1.0)
            v.memset(eps_sb[:, :], EPS)
            v.memset(nhalf_sb[:, :], -0.5)
            v.memset(negc[:, :], -128.0)
            inc(v.drain(), "sV")


    v_const = cnt["sV"]

    ctxA = ExitStack()
    sbA = lambda name, shape, dt: ctxA.enter_context(nc.sbuf_tensor(name, shape, dt))
    uln = sbA("uln", [128, KC * T], BF16)
    xsq = sbA("xsq", [128, 2 * T], BF16)
    tmpn = sbA("tmpn", [128, 1024], BF16)
    t1_sb = sbA("t1_sb", [64, T], BF16)
    st_sb = sbA("st_sb", [128, 2 * T], BF16)
    st2_sb = sbA("st2_sb", [128, 2 * T], BF16)
    t3_sb = sbA("t3_sb", [64, T], BF16)
    kr0 = sbA("kr0", [128, 128], F32)
    kr1 = sbA("kr1", [128, 128], F32)
    if dump_stages:
        k5sd = sbA("k5sd", [32, 32], F32)
    bMb = sbA("bMb", [128, 1024], BF16)
    bRb = sbA("bRb", [128, 1024], BF16)
    mbf0 = sbA("mbf0", [128, 256], BF16)
    mbf1 = sbA("mbf1", [128, 256], BF16)

    def emit_ln(out_tile, with_bc, cast_wait, xbf_on_s=False, t_pre_wait=None,
                xbf_ready=None):
        tT0, tS0, tV0 = cnt["sT"], cnt["sS"], cnt["sV"]
        dt_bc = 1 if with_bc else 0
        vc = 0 if (xbf_on_s or xbf_ready is not None) else KC
        spc = 2 if xbf_on_s else 1
        if with_bc:
            bcM0, bcM1, bcR0, bcR1 = PM0, PM1, PU0, PU1
            sx0, sx1, sq0, sq1 = PM0, PM1, PD0, PD1
        else:
            bcM0, bcM1, bcR0, bcR1 = PM0, PM1, PD0, PD1
            sx0, sx1, sq0, sq1 = PG0, PG1, PM0, PM1
        nrm_base = tV0 + dt_bc + vc + 1 + (1 if with_bc else 0)
        with nc.Block() as blk:
            if with_bc:
                @blk.gpsimd
                def _(g):
                    wt(g, "sK", 1)
                    inc(g.dma_start(kr0[0:32, 0:32], mean_sb[0:1, 0:1024]), "sKd", 16)

            @blk.tensor
            def _(t):
                if with_bc:
                    wt(t, "sdA", cnt["sdA"])
                    wt(t, "sV", v_const)
                    inc(t.matmul(PM0[:, 0:32], ones1[:, :], af_sb[:, :],
                                 start=True, stop=True), "sT")
                    wt(t, "sV", tV0 + 1)
                if t_pre_wait is not None:
                    wt(t, t_pre_wait[0], t_pre_wait[1])
                for k in range(KC):
                    if xbf_ready is not None:
                        wt(t, *xbf_ready(k))
                    elif not xbf_on_s:
                        wt(t, "sV", tV0 + dt_bc + k + 1)
                    wt(t, "sS", tS0 + spc * (k + 1))
                    par = k % 2
                    st, sp = (k == 0), (k == KC - 1)
                    xb = out_tile[:, k * T:(k + 1) * T]
                    t.matmul(sx0[0:1, :], ones128[:, :], xb[:, 0:512],
                             start=st, stop=sp)
                    t.matmul(sx1[0:1, :], ones128[:, :], xb[:, 512:1024],
                             start=st, stop=sp)
                    t.matmul(sq0[0:1, :], ones128[:, :], xsq[:, par * T:par * T + 512],
                             start=st, stop=sp)
                    last = t.matmul(sq1[0:1, :], ones128[:, :],
                                    xsq[:, par * T + 512:par * T + 1024],
                                    start=st, stop=sp)
                    if with_bc:
                        t.matmul(PG0[0:64, :], wbr_sb[:, k * 64:(k + 1) * 64],
                                 xb[:, 0:512], start=st, stop=sp)
                        t.matmul(PG1[0:64, :], wbr_sb[:, k * 64:(k + 1) * 64],
                                 xb[:, 512:1024], start=st, stop=sp)
                        last = t.matmul(PU0[0:64, 0:1],
                                        wbr_sb[:, k * 64:(k + 1) * 64],
                                        ones128[:, 0:1], start=st, stop=sp)
                    inc(last, "sT")
                wt(t, "sV", tV0 + dt_bc + vc + 1)
                wt(t, "sS", tS0 + spc * KC + 1)
                t.matmul(bcM0[:, :], negones[:, :], mean_sb[0:1, 0:512], start=True, stop=True)
                t.matmul(bcM1[:, :], negones[:, :], mean_sb[0:1, 512:1024], start=True, stop=True)
                t.matmul(bcR0[:, :], ones1[:, :], rstd_sb[0:1, 0:512], start=True, stop=True)
                inc(t.matmul(bcR1[:, :], ones1[:, :], rstd_sb[0:1, 512:1024],
                             start=True, stop=True), "sT")

            @blk.scalar
            def _(s):
                for k in range(KC):
                    for nm, vv in cast_wait(k):
                        wt(s, nm, vv)
                    if k >= 2:
                        wt(s, "sT", tT0 + dt_bc + k - 1)
                    inc(s.activation(xsq[:, (k % 2) * T:(k % 2 + 1) * T],
                                     x_sb[:, k * T:(k + 1) * T], AF.Square), "sS")
                    if xbf_on_s:
                        inc(s.copy(xbf[:, (k % 2) * T:(k % 2 + 1) * T],
                                   x_sb[:, k * T:(k + 1) * T]), "sS")
                wt(s, "sV", tV0 + dt_bc + vc + 1)
                s.activation(var_sb[0:1, :], var_sb[0:1, :], AF.Ln,
                             bias=eps_sb[0:1, :])
                s.drain()
                inc(s.activation(rstd_sb[0:1, :], var_sb[0:1, :], AF.Exp,
                                 scale=nhalf_sb[0:1, :]), "sS")

            @blk.vector
            def _(v):
                if with_bc:
                    wt(v, "sT", tT0 + 1)
                    v.tensor_copy(bc_sb[:, :], PM0[:, 0:32])
                    inc(v.drain(), "sV")
                    v.tensor_copy(mean_sb[0:1, 0:4], bc_sb[0:1, 28:32])
                    v.drain()
                    src_f, dst_f = mean_sb, var_sb
                    li = None
                    for tlv in range(1, 5):
                        m = 1 << tlv
                        lvl = 7 - tlv
                        srcv = src_f[0:1, 0:m * m].rearrange("o (r c) -> o r c", c=m)
                        dstv = dst_f[0:1, 0:4 * m * m].rearrange(
                            "o (i r j c) -> o i r j c", i=2, r=m, j=2, c=m)
                        for i in range(2):
                            for j in range(2):
                                col = lvl * 4 + i * 2 + j
                                li = v.tensor_scalar_mul(dstv[0:1, i, :, j, :],
                                                         srcv[0:1, :, :],
                                                         bc_sb[0:1, col:col + 1])
                        v.drain()
                        src_f, dst_f = dst_f, src_f
                    inc(v.drain(), "sK")
                    wt(v, "sKd", 16)
                    if dump_stages:
                        v.tensor_copy(k5sd[:, :], kr0[0:32, 0:32])
                    src2, dst2 = kr0, kr1
                    for tlv in (5, 6):
                        m = 1 << tlv
                        lvl = 7 - tlv
                        for i in range(2):
                            for j in range(2):
                                col = lvl * 4 + i * 2 + j
                                v.tensor_scalar_mul(
                                    dst2[i * m:(i + 1) * m, j * m:(j + 1) * m],
                                    src2[0:m, 0:m], bc_sb[i * m:(i + 1) * m, col:col + 1])
                        v.drain()
                        src2, dst2 = dst2, src2
                    for j in range(2):
                        v.tensor_scalar_mul(mbf0[:, j * 128:(j + 1) * 128], src2[:, :],
                                            bc_sb[:, j:j + 1])
                        v.tensor_scalar_mul(mbf1[:, j * 128:(j + 1) * 128],
                                            src2[:, :], bc_sb[:, 2 + j:2 + j + 1])
                    inc(v.drain(), "sK")
                if not xbf_on_s and xbf_ready is None:
                    for k in range(KC):
                        for nm, vv in cast_wait(k):
                            wt(v, nm, vv)
                        inc(v.tensor_copy(out_tile[:, k * T:(k + 1) * T],
                                          x_sb[:, k * T:(k + 1) * T]), "sV")
                wt(v, "sT", tT0 + dt_bc + KC)
                v.tensor_scalar_mul(mean_sb[0:1, 0:512], sx0[0:1, :], INV_D)
                v.tensor_scalar_mul(mean_sb[0:1, 512:1024], sx1[0:1, :], INV_D)
                v.tensor_scalar_mul(var_sb[0:1, 0:512], sq0[0:1, :], INV_D)
                v.tensor_scalar_mul(var_sb[0:1, 512:1024], sq1[0:1, :], INV_D)
                v.drain()
                v.tensor_mul(rstd_sb[0:1, :], mean_sb[0:1, :], mean_sb[0:1, :])
                if with_bc:
                    v.tensor_copy(wb1_sb[0:64, 0:1], PU0[0:64, 0:1])
                v.drain()
                v.tensor_sub(var_sb[0:1, :], var_sb[0:1, :], rstd_sb[0:1, :])
                inc(v.drain(), "sV")
                wt(v, "sT", tT0 + dt_bc + KC + 1)
                v.tensor_copy(bMb[:, 0:512], bcM0[:, :])
                v.tensor_copy(bMb[:, 512:1024], bcM1[:, :])
                v.tensor_copy(bRb[:, 0:512], bcR0[:, :])
                v.tensor_copy(bRb[:, 512:1024], bcR1[:, :])
                v.drain()
                if with_bc:
                    for hh in range(2):
                        v.scalar_tensor_tensor(
                            tmpn[0:64, hh * 512:(hh + 1) * 512],
                            bMb[0:64, hh * 512:(hh + 1) * 512], wb1_sb[0:64, 0:1],
                            (PG0 if hh == 0 else PG1)[0:64, :], ALU.mult, ALU.add)
                    v.drain()
                    for hh in range(2):
                        v.tensor_mul(t1_sb[0:64, hh * 512:(hh + 1) * 512],
                                     tmpn[0:64, hh * 512:(hh + 1) * 512],
                                     bRb[0:64, hh * 512:(hh + 1) * 512])
                    inc(v.drain(), "sV")
                for k in range(KC):
                    xin_k = (out_tile[:, k * T:(k + 1) * T] if xbf_ready is not None
                             else x_sb[:, k * T:(k + 1) * T])
                    v.tensor_add(tmpn[:, :], xin_k, bMb[:, :])
                    v.drain()
                    v.tensor_mul(out_tile[:, k * T:(k + 1) * T], tmpn[:, :], bRb[:, :])
                    inc(v.drain(), "sV")
        return nrm_base

    ln1_nrm = emit_ln(uln, with_bc=True,
                      cast_wait=lambda k: [(f"sdU{k // 4}", 16)])
    ln1_fix = ln1_nrm
    dump("uln", uln[:, :], [128, KC * T], BF16)
    if dump_stages:
        dump("k5s", k5sd[:, :], [32, 32], F32)
    dump("bc", bc_sb[:, :], [128, 32], F32)
    dump("k7", kr0[:, :], [128, 128], F32)
    dump("k6", kr1[:, :], [128, 128], F32)
    dump("mf0", mbf0[:, :], [128, 256], BF16)
    dump("mf1", mbf1[:, :], [128, 256], BF16)
    dump("mean", mean_sb[0:1, :], [1, T], F32)
    dump("rstd", rstd_sb[0:1, :], [1, T], F32)

    with nc.Block() as blk:
        tT0, tS0, tV0 = cnt["sT"], cnt["sS"], cnt["sV"]

        @blk.tensor
        def _(t):
            wt(t, "sK", 2)
            wt(t, "sV", ln1_fix)
            pmap = {(0, 0): PD0, (0, 1): PD1, (1, 0): PM0, (1, 1): PM1}
            for n0 in range(2):
                for hh in range(2):
                    inc(t.matmul(pmap[(n0, hh)][:, :],
                                 wbs_sb[0:64, n0 * 128:(n0 + 1) * 128],
                                 t1_sb[0:64, hh * 512:(hh + 1) * 512],
                                 start=True, stop=True), "sT")
            wt(t, "sS", tS0 + 4)
            pmap2 = {(0, 0): PG0, (0, 1): PG1, (1, 0): PD0, (1, 1): PD1}
            for j0 in range(2):
                for hh in range(2):
                    for i0 in range(2):
                        mb = mbf0 if i0 == 0 else mbf1
                        last = t.matmul(pmap2[(j0, hh)][:, :],
                                        mb[:, j0 * 128:(j0 + 1) * 128],
                                        st_sb[:, i0 * T + hh * 512:i0 * T + (hh + 1) * 512],
                                        start=(i0 == 0), stop=(i0 == 1))
                    inc(last, "sT")
            wt(t, "sS", tS0 + 8)
            for hh in range(2):
                for j0 in range(2):
                    last = t.matmul((PM0 if hh == 0 else PM1)[0:64, :],
                                    wcr_sb[:, j0 * 64:(j0 + 1) * 64],
                                    st2_sb[:, j0 * T + hh * 512:j0 * T + (hh + 1) * 512],
                                    start=(j0 == 0), stop=(j0 == 1))
                inc(last, "sT")
            wt(t, "sS", tS0 + 10)
            ybanks = (PD0, PD1, PG0, PG1)
            for c in range(KC):
                for hh in range(2):
                    s_ = 2 * c + hh
                    if s_ >= 4:
                        wt(t, "sV", tV0 + s_ - 3)
                    inc(t.matmul(ybanks[s_ % 4][:, :],
                                 wcm_sb[0:64, c * 128:(c + 1) * 128],
                                 t3_sb[0:64, hh * 512:(hh + 1) * 512],
                                 start=True, stop=True), "sT")

        @blk.scalar
        def _(s):
            pmap = {(0, 0): PD0, (0, 1): PD1, (1, 0): PM0, (1, 1): PM1}
            i = 0
            for n0 in range(2):
                for hh in range(2):
                    i += 1
                    wt(s, "sT", tT0 + i)
                    inc(s.copy(st_sb[:, n0 * T + hh * 512:n0 * T + (hh + 1) * 512],
                               pmap[(n0, hh)][:, :]), "sS")
            pmap2 = {(0, 0): PG0, (0, 1): PG1, (1, 0): PD0, (1, 1): PD1}
            i = 0
            for j0 in range(2):
                for hh in range(2):
                    i += 1
                    wt(s, "sT", tT0 + 4 + i)
                    inc(s.copy(st2_sb[:, j0 * T + hh * 512:j0 * T + (hh + 1) * 512],
                               pmap2[(j0, hh)][:, :]), "sS")
            for hh in range(2):
                wt(s, "sT", tT0 + 9 + hh)
                inc(s.copy(t3_sb[0:64, hh * 512:(hh + 1) * 512],
                           (PM0 if hh == 0 else PM1)[0:64, :]), "sS")
            for c in range(KC):
                wt(s, "sV", tV0 + 2 * c + 2)
                inc(s.copy(xln2[:, c * T:(c + 1) * T],
                           x_sb[:, c * T:(c + 1) * T]), "sS")

        @blk.vector
        def _(v):
            wt(v, "sdA", 144)
            ybanks = (PD0, PD1, PG0, PG1)
            for c in range(KC):
                wt(v, "sT", tT0 + 11 + 2 * c + 1)
                for hh in range(2):
                    xs = x_sb[:, c * T + hh * 512:c * T + (hh + 1) * 512]
                    v.tensor_add(xs, xs, ybanks[(2 * c + hh) % 4][:, :])
                v.drain()
                for hh in range(2):
                    xs = x_sb[:, c * T + hh * 512:c * T + (hh + 1) * 512]
                    us = uln[:, c * T + hh * 512:c * T + (hh + 1) * 512]
                    inc(v.scalar_tensor_tensor(xs, us, dv_sb[:, c:c + 1], xs,
                                               ALU.mult, ALU.add), "sV")

    dump("xattn", x_sb[:, :], [128, KC * T], F32)
    dump("t1", t1_sb[0:64, :], [64, T], BF16)
    dump("st", st_sb[:, :], [128, 2 * T], BF16)
    dump("st2", st2_sb[:, :], [128, 2 * T], BF16)
    dump("t3", t3_sb[0:64, :], [64, T], BF16)

    p2vb = cnt["sV"] - 32
    p2sb = cnt["sS"] - 26
    ln2_nrm = emit_ln(xln2, with_bc=False,
                      cast_wait=lambda k: [("sV", p2vb + 2 * k + 2)],
                      xbf_ready=lambda k: ("sS", p2sb + 11 + k))
    dump("xln2", xln2[:, :], [128, KC * T], BF16)

    ctxA.close()
    ctxB = ExitStack()
    sbB = lambda name, shape, dt: ctxB.enter_context(nc.sbuf_tensor(name, shape, dt))
    h_sb = sbB("h_sb", [128, GS * T], BF16)
    sg_sb = sbB("sg_sb", [128, 2 * 512], F32)
    wd_grp = sbB("wd_grp", [128, GS * D], BF16)

    with nc.Block() as blk:
        @blk.gpsimd
        def _(gp):
            for g in range(NG):
                wt(gp, "sPd", g + 1)
                wdv = wd_d[g * GS:(g + 1) * GS, :, :].rearrange("g p h -> p g h")
                wgv = wd_grp[:, :].rearrange("p (g h) -> p g h", h=D)
                inc(gp.dma_start(wgv[:, 0:4, :], wdv[:, 0:4, :], accum_op=ALU.add),
                    "sdD", 16)
                inc(gp.dma_start(wgv[:, 4:8, :], wdv[:, 4:8, :], accum_op=ALU.add),
                    "sdD", 16)
                for mp in range(8 * g + 4, min(8 * g + 12, MC)):
                    wt(gp, "sPw", mp - 3)
                    inc(gp.dma_start(wgbuf[:, (mp % 4) * D:(mp % 4 + 1) * D],
                                     wg_d[mp, :, :], accum_op=ALU.add),
                        f"sdG{mp % 4}", 16)
                    inc(gp.dma_start(wubuf[:, (mp % 4) * D:(mp % 4 + 1) * D],
                                     wu_d[mp, :, :], accum_op=ALU.add),
                        f"sdW{mp % 4}", 16)

        @blk.tensor
        def _(t):
            PDX = (PD0, PD1, PG0, PG1)
            for g in range(NG):
                if g > 0:
                    wt(t, "sVd", 32 * g)
                for ml in range(GS):
                    m = g * GS + ml
                    for hh in range(2):
                        sg_i = 2 * m + hh
                        par = sg_i % 2
                        if sg_i >= 2:
                            wt(t, "sSi", sg_i - 1)
                            wt(t, "sVh", sg_i - 1)
                        if hh == 0:
                            wt(t, f"sdG{m % 4}", 16 * (m // 4 + 1))
                            wt(t, f"sdW{m % 4}", 16 * (m // 4 + 1))
                        first = (g == 0 and ml == 0 and hh == 0)
                        for k in range(KC):
                            if first:
                                wt(t, "sV", ln2_nrm + k + 1)
                            mm = t.matmul(PG[par][:, :],
                                          wgbuf[:, (m % 4) * D + k * 128:(m % 4) * D + (k + 1) * 128],
                                          xln2[:, k * T + hh * 512:k * T + (hh + 1) * 512],
                                          start=(k == 0), stop=(k == KC - 1))
                        inc(mm, "sTg")
                        for k in range(KC):
                            mm = t.matmul(PU[par][:, :],
                                          wubuf[:, (m % 4) * D + k * 128:(m % 4) * D + (k + 1) * 128],
                                          xln2[:, k * T + hh * 512:k * T + (hh + 1) * 512],
                                          start=(k == 0), stop=(k == KC - 1))
                        inc(mm, "sTu")
                wt(t, "sdD", 32 * (g + 1))
                wt(t, "sVh", 16 * (g + 1))
                for c in range(KC):
                    for hh in range(2):
                        sl = 2 * c + hh
                        if sl >= 4:
                            wt(t, "sVd", 32 * g + sl - 3)
                        for ml in range(GS):
                            mm = t.matmul(PDX[sl % 4][:, :],
                                          wd_grp[:, ml * D + c * 128:ml * D + (c + 1) * 128],
                                          h_sb[:, ml * T + hh * 512:ml * T + (hh + 1) * 512],
                                          start=(ml == 0), stop=(ml == GS - 1))
                        inc(mm, "sTd")

        @blk.scalar
        def _(s):
            wt(s, "sdA", 144)
            for g in range(NG):
                for ml in range(GS):
                    m = g * GS + ml
                    for hh in range(2):
                        sg_i = 2 * m + hh
                        par = sg_i % 2
                        wt(s, "sTg", sg_i + 1)
                        inc(s.activation(sg_sb[:, par * 512:(par + 1) * 512],
                                         PG[par][:, :], AF.Silu,
                                         scale=gs_sb[:, m:m + 1]), "sSi")

        @blk.vector
        def _(v):
            PDX = (PD0, PD1, PG0, PG1)
            for g in range(NG):
                if g > 0:
                    wt(v, "sTd", 32 * g)
                for j in range(2 * GS):
                    v.tensor_max(wd_grp[:, j * (D // 2):(j + 1) * (D // 2)],
                                 negc[:, :], negc[:, :])
                inc(v.drain(), "sPd")
                for ml in range(GS):
                    m = g * GS + ml
                    for hh in range(2):
                        sg_i = 2 * m + hh
                        par = sg_i % 2
                        wt(v, "sTu", sg_i + 1)
                        wt(v, "sSi", sg_i + 1)
                        inc(v.scalar_tensor_tensor(
                            h_sb[:, ml * T + hh * 512:ml * T + (hh + 1) * 512],
                            PU[par][:, :], us_sb[:, m:m + 1],
                            sg_sb[:, par * 512:(par + 1) * 512],
                            ALU.mult, ALU.mult), "sVh")
                        if hh == 1 and ml < 6:
                            mp = m + 4
                            if mp < MC:
                                for bb in (wgbuf, wubuf):
                                    for q2 in range(2):
                                        v.tensor_max(
                                            bb[:, (mp % 4) * D + q2 * (D // 2):
                                               (mp % 4) * D + (q2 + 1) * (D // 2)],
                                            negc[:, :], negc[:, :])
                                inc(v.drain(), "sPw")
                for c in range(KC):
                    for hh in range(2):
                        sl = 2 * c + hh
                        wt(v, "sTd", 32 * g + sl + 1)
                        xs = x_sb[:, c * T + hh * 512:c * T + (hh + 1) * 512]
                        inc(v.scalar_tensor_tensor(xs, PDX[sl % 4][:, :],
                                                   ds_sb[:, c:c + 1], xs,
                                                   ALU.mult, ALU.add), "sVd")
                    if c in (1, 3):
                        mp = 8 * g + 10 + (c - 1) // 2
                        if mp < MC:
                            for bb in (wgbuf, wubuf):
                                for q2 in range(2):
                                    v.tensor_max(
                                        bb[:, (mp % 4) * D + q2 * (D // 2):
                                           (mp % 4) * D + (q2 + 1) * (D // 2)],
                                        negc[:, :], negc[:, :])
                            inc(v.drain(), "sPw")
                if g == NG - 1:
                    inc(v.drain(), "sVd")

    ctxB.close()

    with nc.Block() as blk:
        @blk.sync
        def _(y):
            wt(y, "sVd", cnt["sVd"])
            for q in range(4):
                inc(y.dma_start(out_v[:, 4 * q:4 * q + 4, :], x_v[:, 4 * q:4 * q + 4, :]),
                    "sO", 16)
            wt(y, "sO", cnt["sO"])

    ctx.close()
    return nc


_NC_CACHE = None


def kernel(L0_attn_u, L0_attn_A_factors, L0_attn_W_b_rank, L0_attn_W_b_state,
           L0_attn_W_c_rank, L0_attn_W_c_model, L0_attn_D_vec,
           L0_ffn_gate_w, L0_ffn_gate_scale, L0_ffn_up_w, L0_ffn_up_scale,
           L0_ffn_down_w, L0_ffn_down_scale):
    global _NC_CACHE
    u = np.asarray(L0_attn_u)
    uT = np.ascontiguousarray(u[0].T)
    gate = np.asarray(L0_ffn_gate_w)
    up = np.asarray(L0_ffn_up_w)
    down = np.asarray(L0_ffn_down_w)
    def lobyte(a):
        return np.ascontiguousarray(
            a.view(np.uint8).reshape(a.shape + (4,))[..., 0])
    gate_p = lobyte(np.ascontiguousarray(
        gate.T.reshape(KC, 128, MC, 128).transpose(2, 1, 0, 3)).reshape(MC, 128, D))
    up_p = lobyte(np.ascontiguousarray(
        up.T.reshape(KC, 128, MC, 128).transpose(2, 1, 0, 3)).reshape(MC, 128, D))
    down_p = lobyte(np.ascontiguousarray(down.T).reshape(MC, 128, D))
    wbr_p = np.ascontiguousarray(
        np.asarray(L0_attn_W_b_rank).reshape(KC, 128, 64).transpose(1, 0, 2)
    ).reshape(128, KC * 64)
    wbs_p = np.ascontiguousarray(np.asarray(L0_attn_W_b_state))
    wcr_p = np.ascontiguousarray(
        np.asarray(L0_attn_W_c_rank).reshape(2, 128, 64).transpose(1, 0, 2)
    ).reshape(128, 128)
    wcm_p = np.ascontiguousarray(np.asarray(L0_attn_W_c_model))
    dv_p = np.ascontiguousarray(np.asarray(L0_attn_D_vec).reshape(KC, 128).T)
    af_p = np.ascontiguousarray(np.asarray(L0_attn_A_factors).reshape(1, 32))
    gs_p = np.ascontiguousarray(np.asarray(L0_ffn_gate_scale).reshape(MC, 128).T)
    us_p = np.ascontiguousarray(np.asarray(L0_ffn_up_scale).reshape(MC, 128).T)
    ds_p = np.ascontiguousarray(np.asarray(L0_ffn_down_scale).reshape(KC, 128).T)

    if _NC_CACHE is None:
        _NC_CACHE = build_nc()
    nc = _NC_CACHE

    shared = {
        "wg_d": gate_p, "wu_d": up_p, "wd_d": down_p,
        "wbr_d": wbr_p, "wbs_d": wbs_p, "wcr_d": wcr_p, "wcm_d": wcm_p,
        "dv_d": dv_p, "af_d": af_p, "gsc_d": gs_p, "usc_d": us_p, "dsc_d": ds_p,
    }
    in_maps = []
    for c in range(NCORE):
        m = dict(shared)
        m["u_in"] = np.ascontiguousarray(uT[:, c * T:(c + 1) * T])
        in_maps.append(m)

    res = run_bass_kernel_spmd(nc, in_maps, core_ids=list(range(NCORE)))
    outT = np.concatenate([res.results[c]["out_d"] for c in range(NCORE)], axis=1)
    return np.ascontiguousarray(outT.T).reshape(1, 8192, 2048).astype(np.float32)


# version 24
# speedup vs baseline: 1.0150x; 1.0003x over previous
import numpy as np
import concourse.bass as bass
import concourse.mybir as mybir
from concourse.bass_utils import run_bass_kernel_spmd

F32 = mybir.dt.float32
BF16 = mybir.dt.bfloat16
I32 = mybir.dt.int32
AF = mybir.ActivationFunctionType
ALU = mybir.AluOpType

NCORE = 8
T = 1024
KC = 16
D = 2048
MC = 64
GS = 8
NG = MC // GS
EPS = 1e-5
INV_D = 1.0 / D


def build_nc(dump_stages=()):
    nc = bass.Bass("TRN2", target_bir_lowering=False)

    u_in = nc.dram_tensor("u_in", [D, T], F32, kind="ExternalInput")
    U8 = mybir.dt.uint8
    wg_d = nc.dram_tensor("wg_d", [MC, 128, D], U8, kind="ExternalInput")
    wu_d = nc.dram_tensor("wu_d", [MC, 128, D], U8, kind="ExternalInput")
    wd_d = nc.dram_tensor("wd_d", [MC, 128, D], U8, kind="ExternalInput")
    wbr_d = nc.dram_tensor("wbr_d", [128, KC * 64], F32, kind="ExternalInput")
    wbs_d = nc.dram_tensor("wbs_d", [64, 256], F32, kind="ExternalInput")
    wcr_d = nc.dram_tensor("wcr_d", [128, 128], F32, kind="ExternalInput")
    wcm_d = nc.dram_tensor("wcm_d", [64, D], F32, kind="ExternalInput")
    dv_d = nc.dram_tensor("dv_d", [128, KC], F32, kind="ExternalInput")
    af_d = nc.dram_tensor("af_d", [1, 32], F32, kind="ExternalInput")
    gsc_d = nc.dram_tensor("gsc_d", [128, MC], F32, kind="ExternalInput")
    usc_d = nc.dram_tensor("usc_d", [128, MC], F32, kind="ExternalInput")
    dsc_d = nc.dram_tensor("dsc_d", [128, KC], F32, kind="ExternalInput")
    out_d = nc.dram_tensor("out_d", [D, T], F32, kind="ExternalOutput")

    from contextlib import ExitStack
    ctx = ExitStack()
    sb = lambda name, shape, dt: ctx.enter_context(nc.sbuf_tensor(name, shape, dt))
    ps = lambda name: ctx.enter_context(nc.psum_tensor(name, [128, 512], F32))

    x_sb = sb("x_sb", [128, KC * T], F32)
    xln2 = sb("xln2", [128, KC * T], BF16)
    wgbuf = sb("wgbuf", [128, 4 * D], BF16)
    wubuf = sb("wubuf", [128, 4 * D], BF16)
    wbr_sb = sb("wbr_sb", [128, KC * 64], BF16)
    wbs_sb = sb("wbs_sb", [64, 256], BF16)
    wcr_sb = sb("wcr_sb", [128, 128], BF16)
    wcm_sb = sb("wcm_sb", [64, D], BF16)
    dv_sb = sb("dv_sb", [128, KC], F32)
    gs_sb = sb("gs_sb", [128, MC], F32)
    us_sb = sb("us_sb", [128, MC], F32)
    ds_sb = sb("ds_sb", [128, KC], F32)
    af_sb = sb("af_sb", [1, 32], F32)
    bc_sb = sb("bc_sb", [128, 32], F32)
    ones128 = sb("ones128", [128, 1], BF16)
    ones1 = sb("ones1", [1, 128], F32)
    negones = sb("negones", [1, 128], F32)
    wb1_sb = sb("wb1_sb", [64, 1], F32)
    eps_sb = sb("eps_sb", [1, 1], F32)
    nhalf_sb = sb("nhalf_sb", [1, 1], F32)
    mean_sb = sb("mean_sb", [1, T], F32)
    var_sb = sb("var_sb", [1, T], F32)
    rstd_sb = sb("rstd_sb", [1, T], F32)
    negc = sb("negc", [128, D // 2], BF16)

    PG0, PG1 = ps("PG0"), ps("PG1")
    PU0, PU1 = ps("PU0"), ps("PU1")
    PD0, PD1 = ps("PD0"), ps("PD1")
    PM0, PM1 = ps("PM0"), ps("PM1")
    PG = (PG0, PG1)
    PU = (PU0, PU1)
    PD = (PD0, PD1)

    semnames = ["sdU0", "sdU1", "sdU2", "sdU3", "sdA",
                "sdG0", "sdG1", "sdG2", "sdG3", "sdW0", "sdW1", "sdW2", "sdW3",
                "sdD", "sO", "sK", "sKd", "sP0",
                "sT", "sS", "sV", "sTg", "sTu", "sSi", "sVh", "sTd", "sVd",
                "sPw", "sPd"]
    sems = {n: ctx.enter_context(nc.semaphore(n)) for n in semnames}
    cnt = {n: 0 for n in semnames}

    def inc(inst, name, k=1):
        inst.then_inc(sems[name], k)
        cnt[name] += k

    def wt(eng, name, v):
        eng.wait_ge(sems[name], v)

    def dump(name, src_ap, shape, dt):
        if name not in dump_stages:
            return
        d = nc.dram_tensor("dbg_" + name, shape, dt, kind="ExternalOutput")
        with nc.Block() as blk:
            @blk.sync
            def _(y):
                for nm in ("sT", "sS", "sV", "sTg", "sTu", "sSi", "sVh", "sTd", "sVd"):
                    if cnt[nm]:
                        wt(y, nm, cnt[nm])
                inc(y.dma_start(d[tuple(slice(None) for _ in shape)], src_ap), "sO", 16)
                wt(y, "sO", cnt["sO"])

    u_v = u_in[:, :].rearrange("(k p) s -> p k s", p=128)
    out_v = out_d[:, :].rearrange("(k p) s -> p k s", p=128)
    x_v = x_sb[:, :].rearrange("p (k s) -> p k s", s=T)

    with nc.Block() as blk:
        @blk.sync
        def _(y):
            for q in range(4):
                inc(y.dma_start(x_v[:, 4 * q:4 * q + 4, :], u_v[:, 4 * q:4 * q + 4, :]),
                    f"sdU{q}", 16)

        @blk.gpsimd
        def _(g):
            inc(g.dma_start(wbr_sb[:, :], wbr_d[:, :]), "sdA", 16)
            inc(g.dma_start(wbs_sb[:, :], wbs_d[:, :]), "sdA", 16)
            inc(g.dma_start(wcr_sb[:, :], wcr_d[:, :]), "sdA", 16)
            inc(g.dma_start(wcm_sb[:, :], wcm_d[:, :]), "sdA", 16)
            inc(g.dma_start(dv_sb[:, :], dv_d[:, :]), "sdA", 16)
            inc(g.dma_start(af_sb[:, :], af_d[:, :]), "sdA", 16)
            inc(g.dma_start(gs_sb[:, :], gsc_d[:, :]), "sdA", 16)
            inc(g.dma_start(us_sb[:, :], usc_d[:, :]), "sdA", 16)
            inc(g.dma_start(ds_sb[:, :], dsc_d[:, :]), "sdA", 16)
            for m in range(4):
                g.memset(wgbuf[:, (m % 4) * D:(m % 4 + 1) * D], -128.0)
                g.memset(wubuf[:, (m % 4) * D:(m % 4 + 1) * D], -128.0)
            inc(g.drain(), "sP0")
            wt(g, "sP0", 1)
            for m in range(4):
                inc(g.dma_start(wgbuf[:, (m % 4) * D:(m % 4 + 1) * D], wg_d[m, :, :],
                                accum_op=ALU.add), f"sdG{m % 4}", 16)
                inc(g.dma_start(wubuf[:, (m % 4) * D:(m % 4 + 1) * D], wu_d[m, :, :],
                                accum_op=ALU.add), f"sdW{m % 4}", 16)

        @blk.vector
        def _(v):
            v.memset(ones128[:, :], 1.0)
            v.memset(ones1[:, :], 1.0)
            v.memset(negones[:, :], -1.0)
            v.memset(eps_sb[:, :], EPS)
            v.memset(nhalf_sb[:, :], -0.5)
            v.memset(negc[:, :], -128.0)
            inc(v.drain(), "sV")


    v_const = cnt["sV"]

    ctxA = ExitStack()
    sbA = lambda name, shape, dt: ctxA.enter_context(nc.sbuf_tensor(name, shape, dt))
    uln = sbA("uln", [128, KC * T], BF16)
    xsq = sbA("xsq", [128, 2 * T], BF16)
    tmpn = sbA("tmpn", [128, 1024], BF16)
    t1_sb = sbA("t1_sb", [64, T], BF16)
    st_sb = sbA("st_sb", [128, 2 * T], BF16)
    st2_sb = sbA("st2_sb", [128, 2 * T], BF16)
    t3_sb = sbA("t3_sb", [64, T], BF16)
    kr0 = sbA("kr0", [128, 128], F32)
    kr1 = sbA("kr1", [128, 128], F32)
    if dump_stages:
        k5sd = sbA("k5sd", [32, 32], F32)
    bMb = sbA("bMb", [128, 1024], BF16)
    bRb = sbA("bRb", [128, 1024], BF16)
    mbf0 = sbA("mbf0", [128, 256], BF16)
    mbf1 = sbA("mbf1", [128, 256], BF16)

    def emit_ln(out_tile, with_bc, cast_wait, xbf_on_s=False, t_pre_wait=None,
                xbf_ready=None):
        tT0, tS0, tV0 = cnt["sT"], cnt["sS"], cnt["sV"]
        dt_bc = 1 if with_bc else 0
        vc = 0 if (xbf_on_s or xbf_ready is not None) else KC
        spc = 2 if xbf_on_s else 1
        if with_bc:
            bcM0, bcM1, bcR0, bcR1 = PM0, PM1, PU0, PU1
            sx0, sx1, sq0, sq1 = PM0, PM1, PD0, PD1
        else:
            bcM0, bcM1, bcR0, bcR1 = PM0, PM1, PD0, PD1
            sx0, sx1, sq0, sq1 = PG0, PG1, PM0, PM1
        nrm_base = tV0 + dt_bc + vc + 1 + (1 if with_bc else 0)
        with nc.Block() as blk:
            if with_bc:
                @blk.gpsimd
                def _(g):
                    wt(g, "sK", 1)
                    inc(g.dma_start(kr0[0:32, 0:32], mean_sb[0:1, 0:1024]), "sKd", 16)

            @blk.tensor
            def _(t):
                if with_bc:
                    wt(t, "sdA", cnt["sdA"])
                    wt(t, "sV", v_const)
                    inc(t.matmul(PM0[:, 0:32], ones1[:, :], af_sb[:, :],
                                 start=True, stop=True), "sT")
                    wt(t, "sV", tV0 + 1)
                if t_pre_wait is not None:
                    wt(t, t_pre_wait[0], t_pre_wait[1])
                for k in range(KC):
                    if xbf_ready is not None:
                        wt(t, *xbf_ready(k))
                    elif not xbf_on_s:
                        wt(t, "sV", tV0 + dt_bc + k + 1)
                    wt(t, "sS", tS0 + spc * (k + 1))
                    par = k % 2
                    st, sp = (k == 0), (k == KC - 1)
                    xb = out_tile[:, k * T:(k + 1) * T]
                    t.matmul(sx0[0:1, :], ones128[:, :], xb[:, 0:512],
                             start=st, stop=sp)
                    t.matmul(sx1[0:1, :], ones128[:, :], xb[:, 512:1024],
                             start=st, stop=sp)
                    t.matmul(sq0[0:1, :], ones128[:, :], xsq[:, par * T:par * T + 512],
                             start=st, stop=sp)
                    last = t.matmul(sq1[0:1, :], ones128[:, :],
                                    xsq[:, par * T + 512:par * T + 1024],
                                    start=st, stop=sp)
                    if with_bc:
                        t.matmul(PG0[0:64, :], wbr_sb[:, k * 64:(k + 1) * 64],
                                 xb[:, 0:512], start=st, stop=sp)
                        t.matmul(PG1[0:64, :], wbr_sb[:, k * 64:(k + 1) * 64],
                                 xb[:, 512:1024], start=st, stop=sp)
                        last = t.matmul(PU0[0:64, 0:1],
                                        wbr_sb[:, k * 64:(k + 1) * 64],
                                        ones128[:, 0:1], start=st, stop=sp)
                    inc(last, "sT")
                wt(t, "sV", tV0 + dt_bc + vc + 1)
                wt(t, "sS", tS0 + spc * KC + 1)
                t.matmul(bcM0[:, :], negones[:, :], mean_sb[0:1, 0:512], start=True, stop=True)
                t.matmul(bcM1[:, :], negones[:, :], mean_sb[0:1, 512:1024], start=True, stop=True)
                t.matmul(bcR0[:, :], ones1[:, :], rstd_sb[0:1, 0:512], start=True, stop=True)
                inc(t.matmul(bcR1[:, :], ones1[:, :], rstd_sb[0:1, 512:1024],
                             start=True, stop=True), "sT")

            @blk.scalar
            def _(s):
                for k in range(KC):
                    for nm, vv in cast_wait(k):
                        wt(s, nm, vv)
                    if k >= 2:
                        wt(s, "sT", tT0 + dt_bc + k - 1)
                    inc(s.activation(xsq[:, (k % 2) * T:(k % 2 + 1) * T],
                                     x_sb[:, k * T:(k + 1) * T], AF.Square), "sS")
                    if xbf_on_s:
                        inc(s.copy(xbf[:, (k % 2) * T:(k % 2 + 1) * T],
                                   x_sb[:, k * T:(k + 1) * T]), "sS")
                wt(s, "sV", tV0 + dt_bc + vc + 1)
                s.activation(var_sb[0:1, :], var_sb[0:1, :], AF.Ln,
                             bias=eps_sb[0:1, :])
                s.drain()
                inc(s.activation(rstd_sb[0:1, :], var_sb[0:1, :], AF.Exp,
                                 scale=nhalf_sb[0:1, :]), "sS")

            @blk.vector
            def _(v):
                if with_bc:
                    wt(v, "sT", tT0 + 1)
                    v.tensor_copy(bc_sb[:, :], PM0[:, 0:32])
                    inc(v.drain(), "sV")
                    v.tensor_copy(mean_sb[0:1, 0:4], bc_sb[0:1, 28:32])
                    v.drain()
                    src_f, dst_f = mean_sb, var_sb
                    li = None
                    for tlv in range(1, 5):
                        m = 1 << tlv
                        lvl = 7 - tlv
                        srcv = src_f[0:1, 0:m * m].rearrange("o (r c) -> o r c", c=m)
                        dstv = dst_f[0:1, 0:4 * m * m].rearrange(
                            "o (i r j c) -> o i r j c", i=2, r=m, j=2, c=m)
                        for i in range(2):
                            for j in range(2):
                                col = lvl * 4 + i * 2 + j
                                li = v.tensor_scalar_mul(dstv[0:1, i, :, j, :],
                                                         srcv[0:1, :, :],
                                                         bc_sb[0:1, col:col + 1])
                        v.drain()
                        src_f, dst_f = dst_f, src_f
                    inc(v.drain(), "sK")
                    wt(v, "sKd", 16)
                    if dump_stages:
                        v.tensor_copy(k5sd[:, :], kr0[0:32, 0:32])
                    src2, dst2 = kr0, kr1
                    for tlv in (5, 6):
                        m = 1 << tlv
                        lvl = 7 - tlv
                        for i in range(2):
                            for j in range(2):
                                col = lvl * 4 + i * 2 + j
                                v.tensor_scalar_mul(
                                    dst2[i * m:(i + 1) * m, j * m:(j + 1) * m],
                                    src2[0:m, 0:m], bc_sb[i * m:(i + 1) * m, col:col + 1])
                        v.drain()
                        src2, dst2 = dst2, src2
                    for j in range(2):
                        v.tensor_scalar_mul(mbf0[:, j * 128:(j + 1) * 128], src2[:, :],
                                            bc_sb[:, j:j + 1])
                        v.tensor_scalar_mul(mbf1[:, j * 128:(j + 1) * 128],
                                            src2[:, :], bc_sb[:, 2 + j:2 + j + 1])
                    inc(v.drain(), "sK")
                if not xbf_on_s and xbf_ready is None:
                    for k in range(KC):
                        for nm, vv in cast_wait(k):
                            wt(v, nm, vv)
                        inc(v.tensor_copy(out_tile[:, k * T:(k + 1) * T],
                                          x_sb[:, k * T:(k + 1) * T]), "sV")
                wt(v, "sT", tT0 + dt_bc + KC)
                v.tensor_scalar_mul(mean_sb[0:1, 0:512], sx0[0:1, :], INV_D)
                v.tensor_scalar_mul(mean_sb[0:1, 512:1024], sx1[0:1, :], INV_D)
                v.tensor_scalar_mul(var_sb[0:1, 0:512], sq0[0:1, :], INV_D)
                v.tensor_scalar_mul(var_sb[0:1, 512:1024], sq1[0:1, :], INV_D)
                v.drain()
                v.tensor_mul(rstd_sb[0:1, :], mean_sb[0:1, :], mean_sb[0:1, :])
                if with_bc:
                    v.tensor_copy(wb1_sb[0:64, 0:1], PU0[0:64, 0:1])
                v.drain()
                v.tensor_sub(var_sb[0:1, :], var_sb[0:1, :], rstd_sb[0:1, :])
                inc(v.drain(), "sV")
                wt(v, "sT", tT0 + dt_bc + KC + 1)
                v.tensor_copy(bMb[:, 0:512], bcM0[:, :])
                v.tensor_copy(bMb[:, 512:1024], bcM1[:, :])
                v.tensor_copy(bRb[:, 0:512], bcR0[:, :])
                v.tensor_copy(bRb[:, 512:1024], bcR1[:, :])
                v.drain()
                if with_bc:
                    for hh in range(2):
                        v.scalar_tensor_tensor(
                            tmpn[0:64, hh * 512:(hh + 1) * 512],
                            bMb[0:64, hh * 512:(hh + 1) * 512], wb1_sb[0:64, 0:1],
                            (PG0 if hh == 0 else PG1)[0:64, :], ALU.mult, ALU.add)
                    v.drain()
                    for hh in range(2):
                        v.tensor_mul(t1_sb[0:64, hh * 512:(hh + 1) * 512],
                                     tmpn[0:64, hh * 512:(hh + 1) * 512],
                                     bRb[0:64, hh * 512:(hh + 1) * 512])
                    inc(v.drain(), "sV")
                for k in range(KC):
                    xin_k = (out_tile[:, k * T:(k + 1) * T] if xbf_ready is not None
                             else x_sb[:, k * T:(k + 1) * T])
                    v.tensor_add(tmpn[:, :], xin_k, bMb[:, :])
                    v.drain()
                    v.tensor_mul(out_tile[:, k * T:(k + 1) * T], tmpn[:, :], bRb[:, :])
                    inc(v.drain(), "sV")
        return nrm_base

    ln1_nrm = emit_ln(uln, with_bc=True,
                      cast_wait=lambda k: [(f"sdU{k // 4}", 16)])
    ln1_fix = ln1_nrm
    dump("uln", uln[:, :], [128, KC * T], BF16)
    if dump_stages:
        dump("k5s", k5sd[:, :], [32, 32], F32)
    dump("bc", bc_sb[:, :], [128, 32], F32)
    dump("k7", kr0[:, :], [128, 128], F32)
    dump("k6", kr1[:, :], [128, 128], F32)
    dump("mf0", mbf0[:, :], [128, 256], BF16)
    dump("mf1", mbf1[:, :], [128, 256], BF16)
    dump("mean", mean_sb[0:1, :], [1, T], F32)
    dump("rstd", rstd_sb[0:1, :], [1, T], F32)

    with nc.Block() as blk:
        tT0, tS0, tV0 = cnt["sT"], cnt["sS"], cnt["sV"]

        @blk.tensor
        def _(t):
            wt(t, "sK", 2)
            wt(t, "sV", ln1_fix)
            pmap = {(0, 0): PD0, (0, 1): PD1, (1, 0): PM0, (1, 1): PM1}
            for n0 in range(2):
                for hh in range(2):
                    inc(t.matmul(pmap[(n0, hh)][:, :],
                                 wbs_sb[0:64, n0 * 128:(n0 + 1) * 128],
                                 t1_sb[0:64, hh * 512:(hh + 1) * 512],
                                 start=True, stop=True), "sT")
            wt(t, "sS", tS0 + 4)
            pmap2 = {(0, 0): PG0, (0, 1): PG1, (1, 0): PD0, (1, 1): PD1}
            for j0 in range(2):
                for hh in range(2):
                    for i0 in range(2):
                        mb = mbf0 if i0 == 0 else mbf1
                        last = t.matmul(pmap2[(j0, hh)][:, :],
                                        mb[:, j0 * 128:(j0 + 1) * 128],
                                        st_sb[:, i0 * T + hh * 512:i0 * T + (hh + 1) * 512],
                                        start=(i0 == 0), stop=(i0 == 1))
                    inc(last, "sT")
            wt(t, "sS", tS0 + 8)
            for hh in range(2):
                for j0 in range(2):
                    last = t.matmul((PM0 if hh == 0 else PM1)[0:64, :],
                                    wcr_sb[:, j0 * 64:(j0 + 1) * 64],
                                    st2_sb[:, j0 * T + hh * 512:j0 * T + (hh + 1) * 512],
                                    start=(j0 == 0), stop=(j0 == 1))
                inc(last, "sT")
            wt(t, "sS", tS0 + 10)
            ybanks = (PD0, PD1, PG0, PG1)
            for c in range(KC):
                for hh in range(2):
                    s_ = 2 * c + hh
                    if s_ >= 4:
                        wt(t, "sV", tV0 + s_ - 3)
                    inc(t.matmul(ybanks[s_ % 4][:, :],
                                 wcm_sb[0:64, c * 128:(c + 1) * 128],
                                 t3_sb[0:64, hh * 512:(hh + 1) * 512],
                                 start=True, stop=True), "sT")

        @blk.scalar
        def _(s):
            pmap = {(0, 0): PD0, (0, 1): PD1, (1, 0): PM0, (1, 1): PM1}
            i = 0
            for n0 in range(2):
                for hh in range(2):
                    i += 1
                    wt(s, "sT", tT0 + i)
                    inc(s.copy(st_sb[:, n0 * T + hh * 512:n0 * T + (hh + 1) * 512],
                               pmap[(n0, hh)][:, :]), "sS")
            pmap2 = {(0, 0): PG0, (0, 1): PG1, (1, 0): PD0, (1, 1): PD1}
            i = 0
            for j0 in range(2):
                for hh in range(2):
                    i += 1
                    wt(s, "sT", tT0 + 4 + i)
                    inc(s.copy(st2_sb[:, j0 * T + hh * 512:j0 * T + (hh + 1) * 512],
                               pmap2[(j0, hh)][:, :]), "sS")
            for hh in range(2):
                wt(s, "sT", tT0 + 9 + hh)
                inc(s.copy(t3_sb[0:64, hh * 512:(hh + 1) * 512],
                           (PM0 if hh == 0 else PM1)[0:64, :]), "sS")
            for c in range(KC):
                wt(s, "sV", tV0 + 2 * c + 2)
                inc(s.copy(xln2[:, c * T:(c + 1) * T],
                           x_sb[:, c * T:(c + 1) * T]), "sS")

        @blk.vector
        def _(v):
            wt(v, "sdA", 144)
            ybanks = (PD0, PD1, PG0, PG1)
            for c in range(KC):
                wt(v, "sT", tT0 + 11 + 2 * c + 1)
                for hh in range(2):
                    xs = x_sb[:, c * T + hh * 512:c * T + (hh + 1) * 512]
                    v.tensor_add(xs, xs, ybanks[(2 * c + hh) % 4][:, :])
                v.drain()
                for hh in range(2):
                    xs = x_sb[:, c * T + hh * 512:c * T + (hh + 1) * 512]
                    us = uln[:, c * T + hh * 512:c * T + (hh + 1) * 512]
                    inc(v.scalar_tensor_tensor(xs, us, dv_sb[:, c:c + 1], xs,
                                               ALU.mult, ALU.add), "sV")

    dump("xattn", x_sb[:, :], [128, KC * T], F32)
    dump("t1", t1_sb[0:64, :], [64, T], BF16)
    dump("st", st_sb[:, :], [128, 2 * T], BF16)
    dump("st2", st2_sb[:, :], [128, 2 * T], BF16)
    dump("t3", t3_sb[0:64, :], [64, T], BF16)

    p2vb = cnt["sV"] - 32
    p2sb = cnt["sS"] - 26
    ln2_nrm = emit_ln(xln2, with_bc=False,
                      cast_wait=lambda k: [("sV", p2vb + 2 * k + 2)],
                      xbf_ready=lambda k: ("sS", p2sb + 11 + k))
    dump("xln2", xln2[:, :], [128, KC * T], BF16)

    ctxA.close()
    ctxB = ExitStack()
    sbB = lambda name, shape, dt: ctxB.enter_context(nc.sbuf_tensor(name, shape, dt))
    h_sb = sbB("h_sb", [128, GS * T], BF16)
    sg_sb = sbB("sg_sb", [128, 2 * 512], F32)
    wd_grp = sbB("wd_grp", [128, GS * D], BF16)

    with nc.Block() as blk:
        @blk.gpsimd
        def _(gp):
            for g in range(NG):
                wt(gp, "sPd", g + 1)
                wdv = wd_d[g * GS:(g + 1) * GS, :, :].rearrange("g p h -> p g h")
                wgv = wd_grp[:, :].rearrange("p (g h) -> p g h", h=D)
                inc(gp.dma_start(wgv[:, 0:4, :], wdv[:, 0:4, :], accum_op=ALU.add),
                    "sdD", 16)
                inc(gp.dma_start(wgv[:, 4:8, :], wdv[:, 4:8, :], accum_op=ALU.add),
                    "sdD", 16)
                for mp in range(8 * g + 4, min(8 * g + 12, MC)):
                    wt(gp, "sPw", mp - 3)
                    inc(gp.dma_start(wgbuf[:, (mp % 4) * D:(mp % 4 + 1) * D],
                                     wg_d[mp, :, :], accum_op=ALU.add),
                        f"sdG{mp % 4}", 16)
                    inc(gp.dma_start(wubuf[:, (mp % 4) * D:(mp % 4 + 1) * D],
                                     wu_d[mp, :, :], accum_op=ALU.add),
                        f"sdW{mp % 4}", 16)

        @blk.tensor
        def _(t):
            PDX = (PD0, PD1, PG0, PG1)
            for g in range(NG):
                if g > 0:
                    wt(t, "sVd", 32 * g)
                PGx = PG if g == 0 else (PG0, PG1, PD0, PD1)
                PUx = PU if g == 0 else (PU0, PU1, PM0, PM1)
                dep = 2 if g == 0 else 4
                for ml in range(GS):
                    m = g * GS + ml
                    for hh in range(2):
                        sg_i = 2 * m + hh
                        par = sg_i % dep
                        if sg_i - 16 * g >= dep:
                            wt(t, "sVh", sg_i - dep + 1)
                        if hh == 0:
                            wt(t, f"sdG{m % 4}", 16 * (m // 4 + 1))
                            wt(t, f"sdW{m % 4}", 16 * (m // 4 + 1))
                        first = (g == 0 and ml == 0 and hh == 0)
                        for k in range(KC):
                            if first:
                                wt(t, "sV", ln2_nrm + k + 1)
                            mm = t.matmul(PGx[par][:, :],
                                          wgbuf[:, (m % 4) * D + k * 128:(m % 4) * D + (k + 1) * 128],
                                          xln2[:, k * T + hh * 512:k * T + (hh + 1) * 512],
                                          start=(k == 0), stop=(k == KC - 1))
                        inc(mm, "sTg")
                        for k in range(KC):
                            mm = t.matmul(PUx[par][:, :],
                                          wubuf[:, (m % 4) * D + k * 128:(m % 4) * D + (k + 1) * 128],
                                          xln2[:, k * T + hh * 512:k * T + (hh + 1) * 512],
                                          start=(k == 0), stop=(k == KC - 1))
                        inc(mm, "sTu")
                wt(t, "sdD", 32 * (g + 1))
                wt(t, "sVh", 16 * (g + 1))
                for c in range(KC):
                    for hh in range(2):
                        sl = 2 * c + hh
                        if sl >= 4:
                            wt(t, "sVd", 32 * g + sl - 3)
                        for ml in range(GS):
                            mm = t.matmul(PDX[sl % 4][:, :],
                                          wd_grp[:, ml * D + c * 128:ml * D + (c + 1) * 128],
                                          h_sb[:, ml * T + hh * 512:ml * T + (hh + 1) * 512],
                                          start=(ml == 0), stop=(ml == GS - 1))
                        inc(mm, "sTd")

        @blk.scalar
        def _(s):
            wt(s, "sdA", 144)
            for g in range(NG):
                PGx = PG if g == 0 else (PG0, PG1, PD0, PD1)
                dep = 2 if g == 0 else 4
                for ml in range(GS):
                    m = g * GS + ml
                    for hh in range(2):
                        sg_i = 2 * m + hh
                        wt(s, "sTg", sg_i + 1)
                        if g > 0 and sg_i - 16 * g >= 2:
                            wt(s, "sVh", sg_i - 1)
                        inc(s.activation(sg_sb[:, (sg_i % 2) * 512:(sg_i % 2 + 1) * 512],
                                         PGx[sg_i % dep][:, :], AF.Silu,
                                         scale=gs_sb[:, m:m + 1]), "sSi")

        @blk.vector
        def _(v):
            PDX = (PD0, PD1, PG0, PG1)
            for g in range(NG):
                if g > 0:
                    wt(v, "sTd", 32 * g)
                for j in range(2 * GS):
                    v.tensor_max(wd_grp[:, j * (D // 2):(j + 1) * (D // 2)],
                                 negc[:, :], negc[:, :])
                inc(v.drain(), "sPd")
                PUx = PU if g == 0 else (PU0, PU1, PM0, PM1)
                dep = 2 if g == 0 else 4
                for ml in range(GS):
                    m = g * GS + ml
                    for hh in range(2):
                        sg_i = 2 * m + hh
                        wt(v, "sTu", sg_i + 1)
                        wt(v, "sSi", sg_i + 1)
                        inc(v.scalar_tensor_tensor(
                            h_sb[:, ml * T + hh * 512:ml * T + (hh + 1) * 512],
                            PUx[sg_i % dep][:, :], us_sb[:, m:m + 1],
                            sg_sb[:, (sg_i % 2) * 512:(sg_i % 2 + 1) * 512],
                            ALU.mult, ALU.mult), "sVh")
                        if hh == 1 and ml < 6:
                            mp = m + 4
                            if mp < MC:
                                for bb in (wgbuf, wubuf):
                                    for q2 in range(2):
                                        v.tensor_max(
                                            bb[:, (mp % 4) * D + q2 * (D // 2):
                                               (mp % 4) * D + (q2 + 1) * (D // 2)],
                                            negc[:, :], negc[:, :])
                                inc(v.drain(), "sPw")
                for c in range(KC):
                    for hh in range(2):
                        sl = 2 * c + hh
                        wt(v, "sTd", 32 * g + sl + 1)
                        xs = x_sb[:, c * T + hh * 512:c * T + (hh + 1) * 512]
                        inc(v.scalar_tensor_tensor(xs, PDX[sl % 4][:, :],
                                                   ds_sb[:, c:c + 1], xs,
                                                   ALU.mult, ALU.add), "sVd")
                    if c in (1, 3):
                        mp = 8 * g + 10 + (c - 1) // 2
                        if mp < MC:
                            for bb in (wgbuf, wubuf):
                                for q2 in range(2):
                                    v.tensor_max(
                                        bb[:, (mp % 4) * D + q2 * (D // 2):
                                           (mp % 4) * D + (q2 + 1) * (D // 2)],
                                        negc[:, :], negc[:, :])
                            inc(v.drain(), "sPw")
                if g == NG - 1:
                    inc(v.drain(), "sVd")

    ctxB.close()

    with nc.Block() as blk:
        @blk.sync
        def _(y):
            wt(y, "sVd", cnt["sVd"])
            for q in range(4):
                inc(y.dma_start(out_v[:, 4 * q:4 * q + 4, :], x_v[:, 4 * q:4 * q + 4, :]),
                    "sO", 16)
            wt(y, "sO", cnt["sO"])

    ctx.close()
    return nc


_NC_CACHE = None


def kernel(L0_attn_u, L0_attn_A_factors, L0_attn_W_b_rank, L0_attn_W_b_state,
           L0_attn_W_c_rank, L0_attn_W_c_model, L0_attn_D_vec,
           L0_ffn_gate_w, L0_ffn_gate_scale, L0_ffn_up_w, L0_ffn_up_scale,
           L0_ffn_down_w, L0_ffn_down_scale):
    global _NC_CACHE
    u = np.asarray(L0_attn_u)
    uT = np.ascontiguousarray(u[0].T)
    gate = np.asarray(L0_ffn_gate_w)
    up = np.asarray(L0_ffn_up_w)
    down = np.asarray(L0_ffn_down_w)
    def lobyte(a):
        return np.ascontiguousarray(
            a.view(np.uint8).reshape(a.shape + (4,))[..., 0])
    gate_p = lobyte(np.ascontiguousarray(
        gate.T.reshape(KC, 128, MC, 128).transpose(2, 1, 0, 3)).reshape(MC, 128, D))
    up_p = lobyte(np.ascontiguousarray(
        up.T.reshape(KC, 128, MC, 128).transpose(2, 1, 0, 3)).reshape(MC, 128, D))
    down_p = lobyte(np.ascontiguousarray(down.T).reshape(MC, 128, D))
    wbr_p = np.ascontiguousarray(
        np.asarray(L0_attn_W_b_rank).reshape(KC, 128, 64).transpose(1, 0, 2)
    ).reshape(128, KC * 64)
    wbs_p = np.ascontiguousarray(np.asarray(L0_attn_W_b_state))
    wcr_p = np.ascontiguousarray(
        np.asarray(L0_attn_W_c_rank).reshape(2, 128, 64).transpose(1, 0, 2)
    ).reshape(128, 128)
    wcm_p = np.ascontiguousarray(np.asarray(L0_attn_W_c_model))
    dv_p = np.ascontiguousarray(np.asarray(L0_attn_D_vec).reshape(KC, 128).T)
    af_p = np.ascontiguousarray(np.asarray(L0_attn_A_factors).reshape(1, 32))
    gs_p = np.ascontiguousarray(np.asarray(L0_ffn_gate_scale).reshape(MC, 128).T)
    us_p = np.ascontiguousarray(np.asarray(L0_ffn_up_scale).reshape(MC, 128).T)
    ds_p = np.ascontiguousarray(np.asarray(L0_ffn_down_scale).reshape(KC, 128).T)

    if _NC_CACHE is None:
        _NC_CACHE = build_nc()
    nc = _NC_CACHE

    shared = {
        "wg_d": gate_p, "wu_d": up_p, "wd_d": down_p,
        "wbr_d": wbr_p, "wbs_d": wbs_p, "wcr_d": wcr_p, "wcm_d": wcm_p,
        "dv_d": dv_p, "af_d": af_p, "gsc_d": gs_p, "usc_d": us_p, "dsc_d": ds_p,
    }
    in_maps = []
    for c in range(NCORE):
        m = dict(shared)
        m["u_in"] = np.ascontiguousarray(uT[:, c * T:(c + 1) * T])
        in_maps.append(m)

    res = run_bass_kernel_spmd(nc, in_maps, core_ids=list(range(NCORE)))
    outT = np.concatenate([res.results[c]["out_d"] for c in range(NCORE)], axis=1)
    return np.ascontiguousarray(outT.T).reshape(1, 8192, 2048).astype(np.float32)


# version 25
# speedup vs baseline: 1.0284x; 1.0132x over previous
import numpy as np
import concourse.bass as bass
import concourse.mybir as mybir
from concourse.bass_utils import run_bass_kernel_spmd

F32 = mybir.dt.float32
BF16 = mybir.dt.bfloat16
I32 = mybir.dt.int32
AF = mybir.ActivationFunctionType
ALU = mybir.AluOpType

NCORE = 8
T = 1024
KC = 16
D = 2048
MC = 64
GS = 8
NG = MC // GS
EPS = 1e-5
INV_D = 1.0 / D


def build_nc(dump_stages=()):
    nc = bass.Bass("TRN2", target_bir_lowering=False)

    u_in = nc.dram_tensor("u_in", [D, T], F32, kind="ExternalInput")
    U8 = mybir.dt.uint8
    wg_d = nc.dram_tensor("wg_d", [MC, 128, D], U8, kind="ExternalInput")
    wu_d = nc.dram_tensor("wu_d", [MC, 128, D], U8, kind="ExternalInput")
    wd_d = nc.dram_tensor("wd_d", [MC, 128, D], U8, kind="ExternalInput")
    wbr_d = nc.dram_tensor("wbr_d", [128, KC * 64], F32, kind="ExternalInput")
    wbs_d = nc.dram_tensor("wbs_d", [64, 256], F32, kind="ExternalInput")
    wcr_d = nc.dram_tensor("wcr_d", [128, 128], F32, kind="ExternalInput")
    wcm_d = nc.dram_tensor("wcm_d", [64, D], F32, kind="ExternalInput")
    dv_d = nc.dram_tensor("dv_d", [128, KC], F32, kind="ExternalInput")
    af_d = nc.dram_tensor("af_d", [1, 32], F32, kind="ExternalInput")
    gsc_d = nc.dram_tensor("gsc_d", [128, MC], F32, kind="ExternalInput")
    usc_d = nc.dram_tensor("usc_d", [128, MC], F32, kind="ExternalInput")
    dsc_d = nc.dram_tensor("dsc_d", [128, KC], F32, kind="ExternalInput")
    out_d = nc.dram_tensor("out_d", [D, T], F32, kind="ExternalOutput")

    from contextlib import ExitStack
    ctx = ExitStack()
    sb = lambda name, shape, dt: ctx.enter_context(nc.sbuf_tensor(name, shape, dt))
    ps = lambda name: ctx.enter_context(nc.psum_tensor(name, [128, 512], F32))

    x_sb = sb("x_sb", [128, KC * T], F32)
    xln2 = sb("xln2", [128, KC * T], BF16)
    wgbuf = sb("wgbuf", [128, 4 * D], BF16)
    wubuf = sb("wubuf", [128, 4 * D], BF16)
    wbr_sb = sb("wbr_sb", [128, KC * 64], BF16)
    wbs_sb = sb("wbs_sb", [64, 256], BF16)
    wcr_sb = sb("wcr_sb", [128, 128], BF16)
    wcm_sb = sb("wcm_sb", [64, D], BF16)
    dv_sb = sb("dv_sb", [128, KC], F32)
    gs_sb = sb("gs_sb", [128, MC], F32)
    us_sb = sb("us_sb", [128, MC], F32)
    ds_sb = sb("ds_sb", [128, KC], F32)
    af_sb = sb("af_sb", [1, 32], F32)
    bc_sb = sb("bc_sb", [128, 32], F32)
    ones128 = sb("ones128", [128, 1], BF16)
    ones1 = sb("ones1", [1, 128], F32)
    negones = sb("negones", [1, 128], F32)
    wb1_sb = sb("wb1_sb", [64, 1], F32)
    eps_sb = sb("eps_sb", [1, 1], F32)
    nhalf_sb = sb("nhalf_sb", [1, 1], F32)
    mean_sb = sb("mean_sb", [1, T], F32)
    var_sb = sb("var_sb", [1, T], F32)
    rstd_sb = sb("rstd_sb", [1, T], F32)
    negc = sb("negc", [128, D // 2], BF16)

    PG0, PG1 = ps("PG0"), ps("PG1")
    PU0, PU1 = ps("PU0"), ps("PU1")
    PD0, PD1 = ps("PD0"), ps("PD1")
    PM0, PM1 = ps("PM0"), ps("PM1")
    PG = (PG0, PG1)
    PU = (PU0, PU1)
    PD = (PD0, PD1)

    semnames = ["sdU0", "sdU1", "sdU2", "sdU3", "sdA",
                "sdG0", "sdG1", "sdG2", "sdG3", "sdW0", "sdW1", "sdW2", "sdW3",
                "sdD", "sO", "sK", "sKd", "sP0",
                "sT", "sS", "sV", "sTg", "sTu", "sSi", "sVh", "sTd", "sVd",
                "sPw", "sPd"]
    sems = {n: ctx.enter_context(nc.semaphore(n)) for n in semnames}
    cnt = {n: 0 for n in semnames}

    def inc(inst, name, k=1):
        inst.then_inc(sems[name], k)
        cnt[name] += k

    def wt(eng, name, v):
        eng.wait_ge(sems[name], v)

    def dump(name, src_ap, shape, dt):
        if name not in dump_stages:
            return
        d = nc.dram_tensor("dbg_" + name, shape, dt, kind="ExternalOutput")
        with nc.Block() as blk:
            @blk.sync
            def _(y):
                for nm in ("sT", "sS", "sV", "sTg", "sTu", "sSi", "sVh", "sTd", "sVd"):
                    if cnt[nm]:
                        wt(y, nm, cnt[nm])
                inc(y.dma_start(d[tuple(slice(None) for _ in shape)], src_ap), "sO", 16)
                wt(y, "sO", cnt["sO"])

    u_v = u_in[:, :].rearrange("(k p) s -> p k s", p=128)
    out_v = out_d[:, :].rearrange("(k p) s -> p k s", p=128)
    x_v = x_sb[:, :].rearrange("p (k s) -> p k s", s=T)

    with nc.Block() as blk:
        @blk.sync
        def _(y):
            for q in range(4):
                inc(y.dma_start(x_v[:, 4 * q:4 * q + 4, :], u_v[:, 4 * q:4 * q + 4, :]),
                    f"sdU{q}", 16)

        @blk.gpsimd
        def _(g):
            inc(g.dma_start(wbr_sb[:, :], wbr_d[:, :]), "sdA", 16)
            inc(g.dma_start(wbs_sb[:, :], wbs_d[:, :]), "sdA", 16)
            inc(g.dma_start(wcr_sb[:, :], wcr_d[:, :]), "sdA", 16)
            inc(g.dma_start(wcm_sb[:, :], wcm_d[:, :]), "sdA", 16)
            inc(g.dma_start(dv_sb[:, :], dv_d[:, :]), "sdA", 16)
            inc(g.dma_start(af_sb[:, :], af_d[:, :]), "sdA", 16)
            inc(g.dma_start(gs_sb[:, :], gsc_d[:, :]), "sdA", 16)
            inc(g.dma_start(us_sb[:, :], usc_d[:, :]), "sdA", 16)
            inc(g.dma_start(ds_sb[:, :], dsc_d[:, :]), "sdA", 16)
            for m in range(4):
                g.memset(wgbuf[:, (m % 4) * D:(m % 4 + 1) * D], -128.0)
                g.memset(wubuf[:, (m % 4) * D:(m % 4 + 1) * D], -128.0)
            inc(g.drain(), "sP0")
            wt(g, "sP0", 1)
            for m in range(4):
                inc(g.dma_start(wgbuf[:, (m % 4) * D:(m % 4 + 1) * D], wg_d[m, :, :],
                                accum_op=ALU.add), f"sdG{m % 4}", 16)
                inc(g.dma_start(wubuf[:, (m % 4) * D:(m % 4 + 1) * D], wu_d[m, :, :],
                                accum_op=ALU.add), f"sdW{m % 4}", 16)

        @blk.vector
        def _(v):
            v.memset(ones128[:, :], 1.0)
            v.memset(ones1[:, :], 1.0)
            v.memset(negones[:, :], -1.0)
            v.memset(eps_sb[:, :], EPS)
            v.memset(nhalf_sb[:, :], -0.5)
            v.memset(negc[:, :], -128.0)
            inc(v.drain(), "sV")


    v_const = cnt["sV"]

    ctxA = ExitStack()
    sbA = lambda name, shape, dt: ctxA.enter_context(nc.sbuf_tensor(name, shape, dt))
    uln = sbA("uln", [128, KC * T], BF16)
    xsq = sbA("xsq", [128, 2 * T], BF16)
    tmpn = sbA("tmpn", [128, 1024], BF16)
    t1_sb = sbA("t1_sb", [64, T], BF16)
    st_sb = sbA("st_sb", [128, 2 * T], BF16)
    st2_sb = sbA("st2_sb", [128, 2 * T], BF16)
    t3_sb = sbA("t3_sb", [64, T], BF16)
    kr0 = sbA("kr0", [128, 128], F32)
    kr1 = sbA("kr1", [128, 128], F32)
    if dump_stages:
        k5sd = sbA("k5sd", [32, 32], F32)
    bMb = sbA("bMb", [128, 1024], BF16)
    bRb = sbA("bRb", [128, 1024], BF16)
    mbf0 = sbA("mbf0", [128, 256], BF16)
    mbf1 = sbA("mbf1", [128, 256], BF16)

    def emit_ln(out_tile, with_bc, cast_wait, xbf_on_s=False, t_pre_wait=None,
                xbf_ready=None):
        tT0, tS0, tV0 = cnt["sT"], cnt["sS"], cnt["sV"]
        dt_bc = 1 if with_bc else 0
        vc = 0 if (xbf_on_s or xbf_ready is not None) else KC
        spc = 2 if xbf_on_s else 1
        if with_bc:
            bcM0, bcM1, bcR0, bcR1 = PM0, PM1, PU0, PU1
            sx0, sx1, sq0, sq1 = PM0, PM1, PD0, PD1
        else:
            bcM0, bcM1, bcR0, bcR1 = PM0, PM1, PD0, PD1
            sx0, sx1, sq0, sq1 = PG0, PG1, PM0, PM1
        nrm_base = tV0 + dt_bc + vc + 1 + (1 if with_bc else 0)
        with nc.Block() as blk:
            if with_bc:
                @blk.gpsimd
                def _(g):
                    wt(g, "sK", 1)
                    inc(g.dma_start(kr0[0:32, 0:32], mean_sb[0:1, 0:1024]), "sKd", 16)

            @blk.tensor
            def _(t):
                if with_bc:
                    wt(t, "sdA", cnt["sdA"])
                    wt(t, "sV", v_const)
                    inc(t.matmul(PM0[:, 0:32], ones1[:, :], af_sb[:, :],
                                 start=True, stop=True), "sT")
                    wt(t, "sV", tV0 + 1)
                if t_pre_wait is not None:
                    wt(t, t_pre_wait[0], t_pre_wait[1])
                for k in range(KC):
                    if xbf_ready is not None:
                        wt(t, *xbf_ready(k))
                    elif not xbf_on_s:
                        wt(t, "sV", tV0 + dt_bc + k + 1)
                    wt(t, "sS", tS0 + spc * (k + 1))
                    par = k % 2
                    st, sp = (k == 0), (k == KC - 1)
                    xb = out_tile[:, k * T:(k + 1) * T]
                    t.matmul(sx0[0:1, :], ones128[:, :], xb[:, 0:512],
                             start=st, stop=sp)
                    t.matmul(sx1[0:1, :], ones128[:, :], xb[:, 512:1024],
                             start=st, stop=sp)
                    t.matmul(sq0[0:1, :], ones128[:, :], xsq[:, par * T:par * T + 512],
                             start=st, stop=sp)
                    last = t.matmul(sq1[0:1, :], ones128[:, :],
                                    xsq[:, par * T + 512:par * T + 1024],
                                    start=st, stop=sp)
                    if with_bc:
                        t.matmul(PG0[0:64, :], wbr_sb[:, k * 64:(k + 1) * 64],
                                 xb[:, 0:512], start=st, stop=sp)
                        t.matmul(PG1[0:64, :], wbr_sb[:, k * 64:(k + 1) * 64],
                                 xb[:, 512:1024], start=st, stop=sp)
                        last = t.matmul(PU0[0:64, 0:1],
                                        wbr_sb[:, k * 64:(k + 1) * 64],
                                        ones128[:, 0:1], start=st, stop=sp)
                    inc(last, "sT")
                wt(t, "sV", tV0 + dt_bc + vc + 1)
                wt(t, "sS", tS0 + spc * KC + 1)
                t.matmul(bcM0[:, :], negones[:, :], mean_sb[0:1, 0:512], start=True, stop=True)
                t.matmul(bcM1[:, :], negones[:, :], mean_sb[0:1, 512:1024], start=True, stop=True)
                t.matmul(bcR0[:, :], ones1[:, :], rstd_sb[0:1, 0:512], start=True, stop=True)
                inc(t.matmul(bcR1[:, :], ones1[:, :], rstd_sb[0:1, 512:1024],
                             start=True, stop=True), "sT")

            @blk.scalar
            def _(s):
                for k in range(KC):
                    for nm, vv in cast_wait(k):
                        wt(s, nm, vv)
                    if k >= 2:
                        wt(s, "sT", tT0 + dt_bc + k - 1)
                    inc(s.activation(xsq[:, (k % 2) * T:(k % 2 + 1) * T],
                                     x_sb[:, k * T:(k + 1) * T], AF.Square), "sS")
                    if xbf_on_s:
                        inc(s.copy(xbf[:, (k % 2) * T:(k % 2 + 1) * T],
                                   x_sb[:, k * T:(k + 1) * T]), "sS")
                wt(s, "sV", tV0 + dt_bc + vc + 1)
                s.activation(var_sb[0:1, :], var_sb[0:1, :], AF.Ln,
                             bias=eps_sb[0:1, :])
                s.drain()
                inc(s.activation(rstd_sb[0:1, :], var_sb[0:1, :], AF.Exp,
                                 scale=nhalf_sb[0:1, :]), "sS")

            @blk.vector
            def _(v):
                if with_bc:
                    wt(v, "sT", tT0 + 1)
                    v.tensor_copy(bc_sb[:, :], PM0[:, 0:32])
                    inc(v.drain(), "sV")
                    v.tensor_copy(mean_sb[0:1, 0:4], bc_sb[0:1, 28:32])
                    v.drain()
                    src_f, dst_f = mean_sb, var_sb
                    li = None
                    for tlv in range(1, 5):
                        m = 1 << tlv
                        lvl = 7 - tlv
                        srcv = src_f[0:1, 0:m * m].rearrange("o (r c) -> o r c", c=m)
                        dstv = dst_f[0:1, 0:4 * m * m].rearrange(
                            "o (i r j c) -> o i r j c", i=2, r=m, j=2, c=m)
                        for i in range(2):
                            for j in range(2):
                                col = lvl * 4 + i * 2 + j
                                li = v.tensor_scalar_mul(dstv[0:1, i, :, j, :],
                                                         srcv[0:1, :, :],
                                                         bc_sb[0:1, col:col + 1])
                        v.drain()
                        src_f, dst_f = dst_f, src_f
                    inc(v.drain(), "sK")
                    wt(v, "sKd", 16)
                    if dump_stages:
                        v.tensor_copy(k5sd[:, :], kr0[0:32, 0:32])
                    src2, dst2 = kr0, kr1
                    for tlv in (5, 6):
                        m = 1 << tlv
                        lvl = 7 - tlv
                        for i in range(2):
                            for j in range(2):
                                col = lvl * 4 + i * 2 + j
                                v.tensor_scalar_mul(
                                    dst2[i * m:(i + 1) * m, j * m:(j + 1) * m],
                                    src2[0:m, 0:m], bc_sb[i * m:(i + 1) * m, col:col + 1])
                        v.drain()
                        src2, dst2 = dst2, src2
                    for j in range(2):
                        v.tensor_scalar_mul(mbf0[:, j * 128:(j + 1) * 128], src2[:, :],
                                            bc_sb[:, j:j + 1])
                        v.tensor_scalar_mul(mbf1[:, j * 128:(j + 1) * 128],
                                            src2[:, :], bc_sb[:, 2 + j:2 + j + 1])
                    inc(v.drain(), "sK")
                if not xbf_on_s and xbf_ready is None:
                    for k in range(KC):
                        for nm, vv in cast_wait(k):
                            wt(v, nm, vv)
                        inc(v.tensor_copy(out_tile[:, k * T:(k + 1) * T],
                                          x_sb[:, k * T:(k + 1) * T]), "sV")
                wt(v, "sT", tT0 + dt_bc + KC)
                v.tensor_scalar_mul(mean_sb[0:1, 0:512], sx0[0:1, :], INV_D)
                v.tensor_scalar_mul(mean_sb[0:1, 512:1024], sx1[0:1, :], INV_D)
                v.tensor_scalar_mul(var_sb[0:1, 0:512], sq0[0:1, :], INV_D)
                v.tensor_scalar_mul(var_sb[0:1, 512:1024], sq1[0:1, :], INV_D)
                v.drain()
                v.tensor_mul(rstd_sb[0:1, :], mean_sb[0:1, :], mean_sb[0:1, :])
                if with_bc:
                    v.tensor_copy(wb1_sb[0:64, 0:1], PU0[0:64, 0:1])
                v.drain()
                v.tensor_sub(var_sb[0:1, :], var_sb[0:1, :], rstd_sb[0:1, :])
                inc(v.drain(), "sV")
                wt(v, "sT", tT0 + dt_bc + KC + 1)
                v.tensor_copy(bMb[:, 0:512], bcM0[:, :])
                v.tensor_copy(bMb[:, 512:1024], bcM1[:, :])
                v.tensor_copy(bRb[:, 0:512], bcR0[:, :])
                v.tensor_copy(bRb[:, 512:1024], bcR1[:, :])
                v.drain()
                if with_bc:
                    for hh in range(2):
                        v.scalar_tensor_tensor(
                            tmpn[0:64, hh * 512:(hh + 1) * 512],
                            bMb[0:64, hh * 512:(hh + 1) * 512], wb1_sb[0:64, 0:1],
                            (PG0 if hh == 0 else PG1)[0:64, :], ALU.mult, ALU.add)
                    v.drain()
                    for hh in range(2):
                        v.tensor_mul(t1_sb[0:64, hh * 512:(hh + 1) * 512],
                                     tmpn[0:64, hh * 512:(hh + 1) * 512],
                                     bRb[0:64, hh * 512:(hh + 1) * 512])
                    inc(v.drain(), "sV")
                for k in range(KC):
                    xin_k = (out_tile[:, k * T:(k + 1) * T] if xbf_ready is not None
                             else x_sb[:, k * T:(k + 1) * T])
                    v.tensor_add(tmpn[:, :], xin_k, bMb[:, :])
                    v.drain()
                    v.tensor_mul(out_tile[:, k * T:(k + 1) * T], tmpn[:, :], bRb[:, :])
                    inc(v.drain(), "sV")
        return nrm_base

    ln1_nrm = emit_ln(uln, with_bc=True,
                      cast_wait=lambda k: [(f"sdU{k // 4}", 16)])
    ln1_fix = ln1_nrm
    dump("uln", uln[:, :], [128, KC * T], BF16)
    if dump_stages:
        dump("k5s", k5sd[:, :], [32, 32], F32)
    dump("bc", bc_sb[:, :], [128, 32], F32)
    dump("k7", kr0[:, :], [128, 128], F32)
    dump("k6", kr1[:, :], [128, 128], F32)
    dump("mf0", mbf0[:, :], [128, 256], BF16)
    dump("mf1", mbf1[:, :], [128, 256], BF16)
    dump("mean", mean_sb[0:1, :], [1, T], F32)
    dump("rstd", rstd_sb[0:1, :], [1, T], F32)

    with nc.Block() as blk:
        tT0, tS0, tV0 = cnt["sT"], cnt["sS"], cnt["sV"]

        @blk.tensor
        def _(t):
            wt(t, "sK", 2)
            wt(t, "sV", ln1_fix)
            pmap = {(0, 0): PD0, (0, 1): PD1, (1, 0): PM0, (1, 1): PM1}
            for n0 in range(2):
                for hh in range(2):
                    inc(t.matmul(pmap[(n0, hh)][:, :],
                                 wbs_sb[0:64, n0 * 128:(n0 + 1) * 128],
                                 t1_sb[0:64, hh * 512:(hh + 1) * 512],
                                 start=True, stop=True), "sT")
            wt(t, "sS", tS0 + 4)
            pmap2 = {(0, 0): PG0, (0, 1): PG1, (1, 0): PD0, (1, 1): PD1}
            for j0 in range(2):
                for hh in range(2):
                    for i0 in range(2):
                        mb = mbf0 if i0 == 0 else mbf1
                        last = t.matmul(pmap2[(j0, hh)][:, :],
                                        mb[:, j0 * 128:(j0 + 1) * 128],
                                        st_sb[:, i0 * T + hh * 512:i0 * T + (hh + 1) * 512],
                                        start=(i0 == 0), stop=(i0 == 1))
                    inc(last, "sT")
            wt(t, "sS", tS0 + 8)
            for hh in range(2):
                for j0 in range(2):
                    last = t.matmul((PM0 if hh == 0 else PM1)[0:64, :],
                                    wcr_sb[:, j0 * 64:(j0 + 1) * 64],
                                    st2_sb[:, j0 * T + hh * 512:j0 * T + (hh + 1) * 512],
                                    start=(j0 == 0), stop=(j0 == 1))
                inc(last, "sT")
            wt(t, "sS", tS0 + 10)
            ybanks = (PD0, PD1, PG0, PG1)
            for c in range(KC):
                for hh in range(2):
                    s_ = 2 * c + hh
                    if s_ >= 4:
                        wt(t, "sV", tV0 + s_ - 3)
                    inc(t.matmul(ybanks[s_ % 4][:, :],
                                 wcm_sb[0:64, c * 128:(c + 1) * 128],
                                 t3_sb[0:64, hh * 512:(hh + 1) * 512],
                                 start=True, stop=True), "sT")

        @blk.scalar
        def _(s):
            pmap = {(0, 0): PD0, (0, 1): PD1, (1, 0): PM0, (1, 1): PM1}
            i = 0
            for n0 in range(2):
                for hh in range(2):
                    i += 1
                    wt(s, "sT", tT0 + i)
                    inc(s.copy(st_sb[:, n0 * T + hh * 512:n0 * T + (hh + 1) * 512],
                               pmap[(n0, hh)][:, :]), "sS")
            pmap2 = {(0, 0): PG0, (0, 1): PG1, (1, 0): PD0, (1, 1): PD1}
            i = 0
            for j0 in range(2):
                for hh in range(2):
                    i += 1
                    wt(s, "sT", tT0 + 4 + i)
                    inc(s.copy(st2_sb[:, j0 * T + hh * 512:j0 * T + (hh + 1) * 512],
                               pmap2[(j0, hh)][:, :]), "sS")
            for hh in range(2):
                wt(s, "sT", tT0 + 9 + hh)
                inc(s.copy(t3_sb[0:64, hh * 512:(hh + 1) * 512],
                           (PM0 if hh == 0 else PM1)[0:64, :]), "sS")
            for c in range(KC):
                wt(s, "sV", tV0 + 2 * c + 2)
                inc(s.copy(xln2[:, c * T:(c + 1) * T],
                           x_sb[:, c * T:(c + 1) * T]), "sS")

        @blk.vector
        def _(v):
            wt(v, "sdA", 144)
            ybanks = (PD0, PD1, PG0, PG1)
            for c in range(KC):
                wt(v, "sT", tT0 + 11 + 2 * c + 1)
                for hh in range(2):
                    xs = x_sb[:, c * T + hh * 512:c * T + (hh + 1) * 512]
                    v.tensor_add(xs, xs, ybanks[(2 * c + hh) % 4][:, :])
                v.drain()
                for hh in range(2):
                    xs = x_sb[:, c * T + hh * 512:c * T + (hh + 1) * 512]
                    us = uln[:, c * T + hh * 512:c * T + (hh + 1) * 512]
                    inc(v.scalar_tensor_tensor(xs, us, dv_sb[:, c:c + 1], xs,
                                               ALU.mult, ALU.add), "sV")

    dump("xattn", x_sb[:, :], [128, KC * T], F32)
    dump("t1", t1_sb[0:64, :], [64, T], BF16)
    dump("st", st_sb[:, :], [128, 2 * T], BF16)
    dump("st2", st2_sb[:, :], [128, 2 * T], BF16)
    dump("t3", t3_sb[0:64, :], [64, T], BF16)

    p2vb = cnt["sV"] - 32
    p2sb = cnt["sS"] - 26
    ln2_nrm = emit_ln(xln2, with_bc=False,
                      cast_wait=lambda k: [("sV", p2vb + 2 * k + 2)],
                      xbf_ready=lambda k: ("sS", p2sb + 11 + k))
    dump("xln2", xln2[:, :], [128, KC * T], BF16)

    ctxA.close()
    ctxB = ExitStack()
    sbB = lambda name, shape, dt: ctxB.enter_context(nc.sbuf_tensor(name, shape, dt))
    h_sb = sbB("h_sb", [128, GS * T], BF16)
    sg_sb = sbB("sg_sb", [128, 2 * 512], F32)
    wd_grp = sbB("wd_grp", [128, GS * D], BF16)

    with nc.Block() as blk:
        @blk.gpsimd
        def _(gp):
            for g in range(NG):
                wt(gp, "sPd", g + 1)
                wdv = wd_d[g * GS:(g + 1) * GS, :, :].rearrange("g p h -> p g h")
                wgv = wd_grp[:, :].rearrange("p (g h) -> p g h", h=D)
                inc(gp.dma_start(wgv[:, 0:4, :], wdv[:, 0:4, :], accum_op=ALU.add),
                    "sdD", 16)
                inc(gp.dma_start(wgv[:, 4:8, :], wdv[:, 4:8, :], accum_op=ALU.add),
                    "sdD", 16)
                for mp in range(8 * g + 4, min(8 * g + 12, MC)):
                    wt(gp, "sPw", mp - 3)
                    inc(gp.dma_start(wgbuf[:, (mp % 4) * D:(mp % 4 + 1) * D],
                                     wg_d[mp, :, :], accum_op=ALU.add),
                        f"sdG{mp % 4}", 16)
                    inc(gp.dma_start(wubuf[:, (mp % 4) * D:(mp % 4 + 1) * D],
                                     wu_d[mp, :, :], accum_op=ALU.add),
                        f"sdW{mp % 4}", 16)

        @blk.tensor
        def _(t):
            PDX = (PD0, PD1, PG0, PG1)
            for g in range(NG):
                PGx = PG if g == 0 else (PG0, PG1, PD0, PD1)
                PUx = PU if g == 0 else (PU0, PU1, PM0, PM1)
                dep = 2 if g == 0 else 4
                for ml in range(GS):
                    m = g * GS + ml
                    for hh in range(2):
                        sg_i = 2 * m + hh
                        par = sg_i % dep
                        if g > 0 and sg_i - 16 * g == 0:
                            wt(t, "sVd", 32 * g - 1)
                        if g > 0 and sg_i - 16 * g == 1:
                            wt(t, "sVd", 32 * g)
                        if sg_i - 16 * g >= dep:
                            wt(t, "sVh", sg_i - dep + 1)
                        if hh == 0:
                            wt(t, f"sdG{m % 4}", 16 * (m // 4 + 1))
                            wt(t, f"sdW{m % 4}", 16 * (m // 4 + 1))
                        first = (g == 0 and ml == 0 and hh == 0)
                        for k in range(KC):
                            if first:
                                wt(t, "sV", ln2_nrm + k + 1)
                            mm = t.matmul(PGx[par][:, :],
                                          wgbuf[:, (m % 4) * D + k * 128:(m % 4) * D + (k + 1) * 128],
                                          xln2[:, k * T + hh * 512:k * T + (hh + 1) * 512],
                                          start=(k == 0), stop=(k == KC - 1))
                        inc(mm, "sTg")
                        for k in range(KC):
                            mm = t.matmul(PUx[par][:, :],
                                          wubuf[:, (m % 4) * D + k * 128:(m % 4) * D + (k + 1) * 128],
                                          xln2[:, k * T + hh * 512:k * T + (hh + 1) * 512],
                                          start=(k == 0), stop=(k == KC - 1))
                        inc(mm, "sTu")
                wt(t, "sdD", 32 * (g + 1))
                wt(t, "sVh", 16 * g + 15)
                for c in range(KC):
                    for hh in range(2):
                        sl = 2 * c + hh
                        if sl == 1:
                            wt(t, "sVh", 16 * (g + 1))
                        if sl >= 4:
                            wt(t, "sVd", 32 * g + sl - 3)
                        for ml in range(GS):
                            mm = t.matmul(PDX[sl % 4][:, :],
                                          wd_grp[:, ml * D + c * 128:ml * D + (c + 1) * 128],
                                          h_sb[:, ml * T + hh * 512:ml * T + (hh + 1) * 512],
                                          start=(ml == 0), stop=(ml == GS - 1))
                        inc(mm, "sTd")

        @blk.scalar
        def _(s):
            wt(s, "sdA", 144)
            for g in range(NG):
                PGx = PG if g == 0 else (PG0, PG1, PD0, PD1)
                dep = 2 if g == 0 else 4
                for ml in range(GS):
                    m = g * GS + ml
                    for hh in range(2):
                        sg_i = 2 * m + hh
                        wt(s, "sTg", sg_i + 1)
                        if g > 0 and sg_i - 16 * g >= 2:
                            wt(s, "sVh", sg_i - 1)
                        inc(s.activation(sg_sb[:, (sg_i % 2) * 512:(sg_i % 2 + 1) * 512],
                                         PGx[sg_i % dep][:, :], AF.Silu,
                                         scale=gs_sb[:, m:m + 1]), "sSi")

        @blk.vector
        def _(v):
            PDX = (PD0, PD1, PG0, PG1)
            for g in range(NG):
                if g > 0:
                    wt(v, "sTd", 32 * g)
                for j in range(2 * GS):
                    v.tensor_max(wd_grp[:, j * (D // 2):(j + 1) * (D // 2)],
                                 negc[:, :], negc[:, :])
                inc(v.drain(), "sPd")
                PUx = PU if g == 0 else (PU0, PU1, PM0, PM1)
                dep = 2 if g == 0 else 4
                for ml in range(GS):
                    m = g * GS + ml
                    for hh in range(2):
                        sg_i = 2 * m + hh
                        wt(v, "sTu", sg_i + 1)
                        wt(v, "sSi", sg_i + 1)
                        inc(v.scalar_tensor_tensor(
                            h_sb[:, ml * T + hh * 512:ml * T + (hh + 1) * 512],
                            PUx[sg_i % dep][:, :], us_sb[:, m:m + 1],
                            sg_sb[:, (sg_i % 2) * 512:(sg_i % 2 + 1) * 512],
                            ALU.mult, ALU.mult), "sVh")
                        if hh == 1 and ml < 6:
                            mp = m + 4
                            if mp < MC:
                                for bb in (wgbuf, wubuf):
                                    for q2 in range(2):
                                        v.tensor_max(
                                            bb[:, (mp % 4) * D + q2 * (D // 2):
                                               (mp % 4) * D + (q2 + 1) * (D // 2)],
                                            negc[:, :], negc[:, :])
                                inc(v.drain(), "sPw")
                for c in range(KC):
                    for hh in range(2):
                        sl = 2 * c + hh
                        wt(v, "sTd", 32 * g + sl + 1)
                        xs = x_sb[:, c * T + hh * 512:c * T + (hh + 1) * 512]
                        inc(v.scalar_tensor_tensor(xs, PDX[sl % 4][:, :],
                                                   ds_sb[:, c:c + 1], xs,
                                                   ALU.mult, ALU.add), "sVd")
                    if c in (1, 3):
                        mp = 8 * g + 10 + (c - 1) // 2
                        if mp < MC:
                            for bb in (wgbuf, wubuf):
                                for q2 in range(2):
                                    v.tensor_max(
                                        bb[:, (mp % 4) * D + q2 * (D // 2):
                                           (mp % 4) * D + (q2 + 1) * (D // 2)],
                                        negc[:, :], negc[:, :])
                            inc(v.drain(), "sPw")
                if g == NG - 1:
                    inc(v.drain(), "sVd")

    ctxB.close()

    with nc.Block() as blk:
        @blk.sync
        def _(y):
            wt(y, "sVd", cnt["sVd"])
            for q in range(4):
                inc(y.dma_start(out_v[:, 4 * q:4 * q + 4, :], x_v[:, 4 * q:4 * q + 4, :]),
                    "sO", 16)
            wt(y, "sO", cnt["sO"])

    ctx.close()
    return nc


_NC_CACHE = None


def kernel(L0_attn_u, L0_attn_A_factors, L0_attn_W_b_rank, L0_attn_W_b_state,
           L0_attn_W_c_rank, L0_attn_W_c_model, L0_attn_D_vec,
           L0_ffn_gate_w, L0_ffn_gate_scale, L0_ffn_up_w, L0_ffn_up_scale,
           L0_ffn_down_w, L0_ffn_down_scale):
    global _NC_CACHE
    u = np.asarray(L0_attn_u)
    uT = np.ascontiguousarray(u[0].T)
    gate = np.asarray(L0_ffn_gate_w)
    up = np.asarray(L0_ffn_up_w)
    down = np.asarray(L0_ffn_down_w)
    def lobyte(a):
        return np.ascontiguousarray(
            a.view(np.uint8).reshape(a.shape + (4,))[..., 0])
    gate_p = lobyte(np.ascontiguousarray(
        gate.T.reshape(KC, 128, MC, 128).transpose(2, 1, 0, 3)).reshape(MC, 128, D))
    up_p = lobyte(np.ascontiguousarray(
        up.T.reshape(KC, 128, MC, 128).transpose(2, 1, 0, 3)).reshape(MC, 128, D))
    down_p = lobyte(np.ascontiguousarray(down.T).reshape(MC, 128, D))
    wbr_p = np.ascontiguousarray(
        np.asarray(L0_attn_W_b_rank).reshape(KC, 128, 64).transpose(1, 0, 2)
    ).reshape(128, KC * 64)
    wbs_p = np.ascontiguousarray(np.asarray(L0_attn_W_b_state))
    wcr_p = np.ascontiguousarray(
        np.asarray(L0_attn_W_c_rank).reshape(2, 128, 64).transpose(1, 0, 2)
    ).reshape(128, 128)
    wcm_p = np.ascontiguousarray(np.asarray(L0_attn_W_c_model))
    dv_p = np.ascontiguousarray(np.asarray(L0_attn_D_vec).reshape(KC, 128).T)
    af_p = np.ascontiguousarray(np.asarray(L0_attn_A_factors).reshape(1, 32))
    gs_p = np.ascontiguousarray(np.asarray(L0_ffn_gate_scale).reshape(MC, 128).T)
    us_p = np.ascontiguousarray(np.asarray(L0_ffn_up_scale).reshape(MC, 128).T)
    ds_p = np.ascontiguousarray(np.asarray(L0_ffn_down_scale).reshape(KC, 128).T)

    if _NC_CACHE is None:
        _NC_CACHE = build_nc()
    nc = _NC_CACHE

    shared = {
        "wg_d": gate_p, "wu_d": up_p, "wd_d": down_p,
        "wbr_d": wbr_p, "wbs_d": wbs_p, "wcr_d": wcr_p, "wcm_d": wcm_p,
        "dv_d": dv_p, "af_d": af_p, "gsc_d": gs_p, "usc_d": us_p, "dsc_d": ds_p,
    }
    in_maps = []
    for c in range(NCORE):
        m = dict(shared)
        m["u_in"] = np.ascontiguousarray(uT[:, c * T:(c + 1) * T])
        in_maps.append(m)

    res = run_bass_kernel_spmd(nc, in_maps, core_ids=list(range(NCORE)))
    outT = np.concatenate([res.results[c]["out_d"] for c in range(NCORE)], axis=1)
    return np.ascontiguousarray(outT.T).reshape(1, 8192, 2048).astype(np.float32)
